# version 13
# speedup vs baseline: 2.0961x; 1.0058x over previous
import numpy as np
import ml_dtypes
from contextlib import ExitStack

import concourse.bass as bass
import concourse.mybir as mybir
from concourse.bass_utils import run_bass_kernel_spmd
from concourse.alu_op_type import AluOpType as OP

F32 = mybir.dt.float32
BF16 = mybir.dt.bfloat16
AF = mybir.ActivationFunctionType
AX = mybir.AxisListType

R = 8
T = 2048
C = 8
D = 1024
PIECE = 512
NP = T // PIECE
PAIRS = 8
NCH = T // 128
DFF = 4096
NJ = 8
NE = 8
NEG = -1.0e30
LN_EPS = 1e-5
RMS_EPS = 1e-6

G4LIM = 9
DEBUG_TAPS = ()
BUILD_PHASES = 7


def build_nc():
    nc = bass.Bass("TRN2", target_bir_lowering=False, detect_race_conditions=False)

    def IN(name, shape, dt=F32):
        return nc.dram_tensor(name, list(shape), dt, kind="ExternalInput").ap()

    x_f = IN("x_f", [D, T])
    wgt = IN("wgt", [D, D]); wvt = IN("wvt", [D, D]); wdt = IN("wdt", [D, D])
    wqkvt = IN("wqkvt", [D, 3 * D])
    wot = IN("wot", [D, D])
    wgatet = IN("wgatet", [D, DFF]); wupt = IN("wupt", [D, DFF])
    woutt = IN("woutt", [DFF, D])
    wemt = IN("wemt", [NE, D, D])
    wgm = IN("wgm", [D, NE])
    wmix = IN("wmix", [128, C]); wattn = IN("wattn", [128, C]); wffn = IN("wffn", [128, C])
    bg3 = IN("bg3", [128, C]); bv3 = IN("bv3", [128, C]); bd3 = IN("bd3", [128, C])
    bgm = IN("bgm", [NE, 1])
    bem = IN("bem", [1, NE * D])
    eye_bf = IN("eye_bf", [128, 128], BF16)
    eye_f = IN("eye_f", [128, 128])
    mask_std = IN("mask_std", [128, 512], BF16)
    mask_first = IN("mask_first", [128, 512], BF16)
    sel_carry = IN("sel_carry", [128, R])
    sel_halo = IN("sel_halo", [128, R])

    out_f = nc.dram_tensor("out_f", [D, T], F32, kind="ExternalOutput").ap()
    taps = {t: nc.dram_tensor("tap_" + t, [D, T], F32, kind="ExternalOutput").ap()
            for t in DEBUG_TAPS}

    cc_sc_in = [nc.dram_tensor(f"cc_sc_in{c}", [128, 2], F32).ap() for c in range(C)]
    cc_sc_out = [nc.dram_tensor(f"cc_sc_out{c}", [R * 128, 2], F32).ap() for c in range(C)]
    cc_kv_in = nc.dram_tensor("cc_kv_in", [128, 2 * PAIRS * 128], BF16).ap()
    cc_kv_out = nc.dram_tensor("cc_kv_out", [R * 128, 2 * PAIRS * 128], BF16).ap()

    ctx = ExitStack()
    E = ctx.enter_context

    xf = E(nc.sbuf_tensor("xf", [128, C, T], F32))
    opnd = E(nc.sbuf_tensor("opnd", [128, C, T], BF16))
    sbf = E(nc.sbuf_tensor("sbf", [128, C, T], BF16))
    wmix_t = E(nc.sbuf_tensor("wmix_t", [128, C], F32))
    wattn_t = E(nc.sbuf_tensor("wattn_t", [128, C], F32))
    wffn_t = E(nc.sbuf_tensor("wffn_t", [128, C], F32))
    bg_t = E(nc.sbuf_tensor("bg_t", [128, C], F32))
    bv_t = E(nc.sbuf_tensor("bv_t", [128, C], F32))
    bd_t = E(nc.sbuf_tensor("bd_t", [128, C], F32))
    bgm_t = E(nc.sbuf_tensor("bgm_t", [NE, 1], F32))
    eyeb_t = E(nc.sbuf_tensor("eyeb_t", [128, 128], BF16))
    eyef_t = E(nc.sbuf_tensor("eyef_t", [128, 128], F32))
    mstd_t = E(nc.sbuf_tensor("mstd_t", [128, 512], BF16))
    mfirst_t = E(nc.sbuf_tensor("mfirst_t", [128, 512], BF16))
    selc_t = E(nc.sbuf_tensor("selc_t", [128, R], F32))
    selh_t = E(nc.sbuf_tensor("selh_t", [128, R], F32))
    ones_r = E(nc.sbuf_tensor("ones_r", [1, 128], F32))
    ones_b = E(nc.sbuf_tensor("ones_b", [128, 1], BF16))
    eps1_t = E(nc.sbuf_tensor("eps1_t", [1, 1], F32))
    eps2_t = E(nc.sbuf_tensor("eps2_t", [1, 1], F32))
    ones_rb = E(nc.sbuf_tensor("ones_rb", [1, 128], BF16))
    _pw = ExitStack()
    wg_s = _pw.enter_context(nc.sbuf_tensor("wg_s", [128, 2, C, 128], BF16))
    wv_s = _pw.enter_context(nc.sbuf_tensor("wv_s", [128, 2, C, 128], BF16))
    wd_s = _pw.enter_context(nc.sbuf_tensor("wd_s", [128, 2, C, 128], BF16))

    block = E(nc.Block())

    SEMS = ("s_x", "s_w0", "s_w1", "s_pe", "s_act", "s_dve",
            "s_cc", "s_g0", "s_g1", "s_q", "s_st", "s_pool", "s_pub")
    S = {n: E(nc.semaphore(n)) for n in SEMS}
    cnt = {n: 0 for n in SEMS}

    def inc(inst, sn, n=1):
        inst.then_inc(S[sn], n)
        cnt[sn] += n
        return cnt[sn]

    def dma(eng, sn, out, in_):
        return inc(eng.dma_start(out=out, in_=in_), sn, 16)

    def w8(eng, sn, v):
        if not _pass2[0]:
            assert v <= cnt[sn], (sn, v, cnt[sn])
        if v > 0:
            eng.wait_ge(S[sn], v)

    _NOREC = {"wait_ge", "nop", "dma_start", "sem_inc", "sem_clear", "drain",
              "dma_start_transpose"}

    class _EngRec:
        def __init__(self, eng):
            self._e = eng
            self.last = None
        def __getattr__(self, n):
            f = getattr(self._e, n)
            if callable(f) and not n.startswith("_"):
                rec = self
                def w(*a, _f=f, _n=n, **k):
                    r = _f(*a, **k)
                    if _n not in _NOREC and hasattr(r, "then_inc"):
                        rec.last = r
                    return r
                return w
            return f

    class _MockInst:
        def then_inc(self, *a, **k):
            return self

    _MOCK_INST = _MockInst()

    class _MockEng:
        last = _MOCK_INST
        def __getattr__(self, n):
            return lambda *a, **k: _MOCK_INST

    _MOCK = _MockEng()

    _stage_buf = []
    _span_cnt = [dict(cnt)]
    _pass2 = [False]

    def stage(engn, fn):
        _stage_buf.append((engn, fn))
        fn(_MOCK)

    sec = stage

    def flush():
        if not _stage_buf:
            return
        end_cnt = dict(cnt)
        cnt.update(_span_cnt[0])
        _pass2[0] = True
        for engn in ("sync", "gpsimd", "tensor", "scalar", "vector"):
            fns = [f for e, f in _stage_buf if e == engn]
            if not fns:
                continue
            def body(eng, fns=fns):
                w = _EngRec(eng)
                for f in fns:
                    f(w)
            getattr(block, engn)(body)
        _pass2[0] = False
        assert cnt == end_cnt, (cnt, end_cnt)
        _stage_buf.clear()
        _span_cnt[0] = dict(cnt)

    def incl(engw, sn, n=1):
        assert engw.last is not None, "incl with no prior real instruction"
        return inc(engw.last, sn, n)

    def _(sy):
        dma(sy, "s_x", wmix_t[:, :], wmix)
        dma(sy, "s_x", wattn_t[:, :], wattn)
        dma(sy, "s_x", wffn_t[:, :], wffn)
        dma(sy, "s_x", bg_t[:, :], bg3)
        dma(sy, "s_x", bv_t[:, :], bv3)
        dma(sy, "s_x", bd_t[:, :], bd3)
        dma(sy, "s_x", bgm_t[:, :], bgm)
        dma(sy, "s_x", eyeb_t[:, :], eye_bf)
        dma(sy, "s_x", eyef_t[:, :], eye_f)
        dma(sy, "s_x", mstd_t[:, :], mask_std)
        dma(sy, "s_x", mfirst_t[:, :], mask_first)
        dma(sy, "s_x", selc_t[:, :], sel_carry)
        dma(sy, "s_x", selh_t[:, :], sel_halo)
        for c in range(C):
            dma(sy, "s_x", xf[:, c, :], x_f[128 * c:128 * (c + 1), :])
    sec("sync", _)
    mark_const = cnt["s_x"]

    def _(v):
        v.memset(ones_r[:, :], 1.0)
        v.memset(ones_b[:, :], 1.0)
        v.memset(eps1_t[:, :], LN_EPS)
        v.memset(eps2_t[:, :], RMS_EPS)
        v.memset(ones_rb[:, :], 1.0)
        incl(v, "s_dve")
    sec("vector", _)
    mark_ones = cnt["s_dve"]
    flush()

    def tap_xf(name, prev_dve_mark):
        if name not in taps:
            return 0
        def _(g):
            w8(g, "s_dve", prev_dve_mark)
            for c in range(C):
                dma(g, "s_st", taps[name][128 * c:128 * (c + 1), :], xf[:, c, :])
        sec("gpsimd", _)
        return cnt["s_st"]

    def tap_opnd(name, prev_dve_mark):
        if name not in taps:
            return 0
        def _(g):
            w8(g, "s_dve", prev_dve_mark)
            for c in range(C):
                dma(g, "s_st", taps[name][128 * c:128 * (c + 1), :], opnd[:, c, :])
        sec("gpsimd", _)
        return cnt["s_st"]


    def _finish(done_mark):
        flush()
        def _(sy):
            w8(sy, "s_dve", done_mark)
            m = 0
            for c in range(C):
                m = dma(sy, "s_st", out_f[128 * c:128 * (c + 1), :], xf[:, c, :])
            w8(sy, "s_st", m)
        sec("sync", _)
        flush()
        ctx.close()
        return nc

    if BUILD_PHASES <= 0:
        return _finish(mark_ones)

    ph1 = {}
    _p1 = ExitStack()
    ps_mu = _p1.enter_context(nc.psum_tensor("ps_mu", [1, 2, PIECE], F32))
    ps_ms = _p1.enter_context(nc.psum_tensor("ps_ms", [1, 2, PIECE], F32))
    ps_b1 = _p1.enter_context(nc.psum_tensor("ps_b1", [128, PIECE], F32))
    ps_b2 = _p1.enter_context(nc.psum_tensor("ps_b2", [128, PIECE], F32))
    ps_b3 = _p1.enter_context(nc.psum_tensor("ps_b3", [128, PIECE], F32))
    cast_s = _p1.enter_context(nc.sbuf_tensor("cast_s", [128, 2, PIECE], BF16))
    sq_s = _p1.enter_context(nc.sbuf_tensor("sq_s", [128, 2, PIECE], BF16))
    tmp_s = _p1.enter_context(nc.sbuf_tensor("tmp_s", [128, PIECE], F32))
    rows = _p1.enter_context(nc.sbuf_tensor("rows1", [1, 5, PIECE], F32))
    if True:

        pe_stats = [0] * NP
        dve_var = [0] * NP
        pe_bc = [0] * NP
        dve_apply = [0] * NP
        pe_cs = {}
        act_cs = {}

        for p in range(NP):
            sl = slice(p * PIECE, (p + 1) * PIECE)
            pp = p % 2

            for c in range(C):
                cs = p * C + c

                def _(s, p=p, c=c, cs=cs, sl=sl):
                    if cs == 0:
                        w8(s, "s_x", mark_const)
                        w8(s, "s_dve", mark_ones)
                    if cs >= 2:
                        w8(s, "s_pe", pe_cs[cs - 2])
                    s.activation(out=cast_s[:, cs % 2, :], in_=xf[:, c, sl], func=AF.Copy)
                    s.activation(out=sq_s[:, cs % 2, :], in_=xf[:, c, sl], func=AF.Square)
                    act_cs[cs] = incl(s, "s_act")
                sec("scalar", _)

                def _(t, p=p, c=c, cs=cs, pp=pp):
                    w8(t, "s_act", act_cs[cs])
                    if c == 0:
                        w8(t, "s_dve", mark_ones)
                        if p >= 2:
                            w8(t, "s_dve", dve_var[p - 2])
                    t.matmul(ps_mu[0:1, pp, :], ones_b[:, :], cast_s[:, cs % 2, :],
                             start=(c == 0), stop=(c == C - 1))
                    t.matmul(ps_ms[0:1, pp, :], ones_b[:, :], sq_s[:, cs % 2, :],
                             start=(c == 0), stop=(c == C - 1))
                    pe_cs[cs] = incl(t, "s_pe")
                pe_stats[p] = None
                sec("tensor", _)
            pe_stats[p] = cnt["s_pe"]

            def _(v, p=p, pp=pp):
                w8(v, "s_pe", pe_stats[p])
                if p >= 1:
                    w8(v, "s_pe", pe_bc[p - 1])
                v.tensor_scalar(rows[0:1, 0, :], ps_mu[0:1, pp, :], 1.0 / D, None, OP.mult)
                v.tensor_scalar(rows[0:1, 4, :], ps_ms[0:1, pp, :], 1.0 / D, None, OP.mult)
                v.tensor_tensor(rows[0:1, 1, :], rows[0:1, 0, :], rows[0:1, 0, :], OP.mult)
                v.tensor_tensor(rows[0:1, 1, :], rows[0:1, 4, :], rows[0:1, 1, :], OP.subtract)
                dve_var[p] = incl(v, "s_dve")
            sec("vector", _)

            def _(s, p=p):
                w8(s, "s_dve", dve_var[p])
                s.activation(out=rows[0:1, 4, :], in_=rows[0:1, 1, :], func=AF.Ln, bias=eps1_t[0:1, :])
                s.drain()
                s.activation(out=rows[0:1, 2, :], in_=rows[0:1, 4, :], func=AF.Exp, scale=-0.5)
                incl(s, "s_act")
            sec("scalar", _)
            m_r1 = cnt["s_act"]

            def _(v, m_r1=m_r1):
                w8(v, "s_act", m_r1)
                v.tensor_tensor(rows[0:1, 0, :], rows[0:1, 0, :], rows[0:1, 2, :], OP.mult)
                v.tensor_tensor(rows[0:1, 4, :], rows[0:1, 2, :], rows[0:1, 2, :], OP.mult)
                v.tensor_tensor(rows[0:1, 3, :], rows[0:1, 1, :], rows[0:1, 4, :], OP.mult)
                incl(v, "s_dve")
            sec("vector", _)
            m_ms2 = cnt["s_dve"]

            def _(s, m_ms2=m_ms2):
                w8(s, "s_dve", m_ms2)
                s.activation(out=rows[0:1, 3, :], in_=rows[0:1, 3, :], func=AF.Ln, bias=eps2_t[0:1, :])
                s.drain()
                s.activation(out=rows[0:1, 3, :], in_=rows[0:1, 3, :], func=AF.Exp, scale=-0.5)
                incl(s, "s_act")
            sec("scalar", _)
            m_r2 = cnt["s_act"]

            def _(t, p=p, m_r2=m_r2):
                w8(t, "s_act", m_r2)
                if p >= 1:
                    w8(t, "s_dve", dve_apply[p - 1])
                t.matmul(ps_b1[:, :], ones_r[:, :], rows[0:1, 2, :], start=True, stop=True)
                t.matmul(ps_b2[:, :], ones_r[:, :], rows[0:1, 0, :], start=True, stop=True)
                t.matmul(ps_b3[:, :], ones_r[:, :], rows[0:1, 3, :], start=True, stop=True)
                pe_bc[p] = incl(t, "s_pe")
            sec("tensor", _)

            def _(v, p=p, sl=sl):
                w8(v, "s_pe", pe_bc[p])
                for c in range(C):
                    v.tensor_tensor(tmp_s[:, :], xf[:, c, sl], ps_b1[:, :], OP.mult)
                    v.tensor_tensor(xf[:, c, sl], tmp_s[:, :], ps_b2[:, :], OP.subtract)
                    v.tensor_tensor(tmp_s[:, :], xf[:, c, sl], ps_b3[:, :], OP.mult)
                    v.tensor_scalar(opnd[:, c, sl], tmp_s[:, :], wmix_t[:, c:c + 1], None, OP.mult)
                dve_apply[p] = incl(v, "s_dve")
            sec("vector", _)

        ph1["done"] = dve_apply[NP - 1]
        ph1["applies"] = list(dve_apply)
    flush()
    _p1.close()

    ln_done = ph1["done"]
    tg = max(tap_xf("xn", ln_done), tap_opnd("xm", ln_done))
    if BUILD_PHASES <= 1:
        return _finish(ln_done)

    ph2 = {}
    _p2 = ExitStack()
    ps_g = _p2.enter_context(nc.psum_tensor("ps_g", [128, 2, PIECE], F32))
    ps_v = _p2.enter_context(nc.psum_tensor("ps_v", [128, 2, PIECE], F32))
    ps_d = _p2.enter_context(nc.psum_tensor("ps_d", [128, 2, PIECE], F32))
    a_s = _p2.enter_context(nc.sbuf_tensor("a_s", [128, 2, T], F32))
    xs_s = _p2.enter_context(nc.sbuf_tensor("xs_s", [128, 2, T], F32))
    sg_s = _p2.enter_context(nc.sbuf_tensor("sg_s", [128, PIECE], F32))
    tv_s = _p2.enter_context(nc.sbuf_tensor("tv_s", [128, PIECE], F32))
    dum_s = _p2.enter_context(nc.sbuf_tensor("dum_s", [128, PIECE], F32))
    h_s = _p2.enter_context(nc.sbuf_tensor("h_s", [128, T], F32))
    pub_s = _p2.enter_context(nc.sbuf_tensor("pub_s", [128, 2, 2], F32))
    gath_s = _p2.enter_context(nc.sbuf_tensor("gath_s", [128, 2, 2 * R], F32))
    st_s = _p2.enter_context(nc.sbuf_tensor("st_s", [128, 4], F32))
    ls_s = _p2.enter_context(nc.sbuf_tensor("ls_s", [128, 8], F32))
    if True:

        wsem = ["s_w0", "s_w1"]
        gsem = ["s_g0", "s_g1"]
        pe_gvd = [[0] * NP for _ in range(C)]
        act_gvd = [[0] * NP for _ in range(C)]
        dve_axs = [[0] * NP for _ in range(C)]
        dve_scan1 = [0] * C
        act_atot = [0] * C
        dve_y1 = [0] * C
        gath_mark = [0] * C
        cc_mark = [0] * C
        w_mark = [0] * C

        def emit_tail(c):
            cb = c % 2

            def _(v, c=c, cb=cb):
                w8(v, gsem[cb], gath_mark[c])
                v.memset(st_s[:, 0:1], 0.0)
                v.memset(st_s[:, 2:3], 0.0)
                cur = 0
                for j in range(R):
                    v.scalar_tensor_tensor(st_s[:, 2:3], st_s[:, cur:cur + 1],
                                           selc_t[:, j:j + 1], st_s[:, 2:3],
                                           OP.mult, OP.add)
                    if j < R - 1:
                        nxt = 1 - cur
                        v.scalar_tensor_tensor(st_s[:, nxt:nxt + 1], st_s[:, cur:cur + 1],
                                               gath_s[:, cb, 2 * j:2 * j + 1],
                                               gath_s[:, cb, 2 * j + 1:2 * j + 2],
                                               OP.mult, OP.add)
                        cur = nxt
                v.drain()
                v.tensor_tensor_scan(h_s[:, :], a_s[:, cb, :], xs_s[:, cb, :],
                                     st_s[:, 2:3], OP.mult, OP.add)
                v.drain()
                v.tensor_tensor(xf[:, c, :], xf[:, c, :], h_s[:, :], OP.add)
                dve_y1[c] = incl(v, "s_dve")
            sec("vector", _)

        def stage_gru_w(c):
            cb = c % 2
            ws = wsem[cb]

            def _(g, c=c, cb=cb):
                if c >= 2:
                    w8(g, "s_pe", pe_gvd[c - 2][NP - 1])
                dma(g, ws, wg_s[:, cb, :, :],
                    wgt[:, 128 * c:128 * (c + 1)].rearrange("(k p) m -> p k m", p=128))
                dma(g, ws, wv_s[:, cb, :, :],
                    wvt[:, 128 * c:128 * (c + 1)].rearrange("(k p) m -> p k m", p=128))
                dma(g, ws, wd_s[:, cb, :, :],
                    wdt[:, 128 * c:128 * (c + 1)].rearrange("(k p) m -> p k m", p=128))
            sec("gpsimd", _)
            w_mark[c] = cnt[ws]

        stage_gru_w(0)
        stage_gru_w(1)
        for c in range(C):
            cb = c % 2
            ws = wsem[cb]
            gs = gsem[cb]

            for p in range(NP):
                sl = slice(p * PIECE, (p + 1) * PIECE)
                pp = p % 2
                step = c * NP + p

                def _(t, c=c, cb=cb, p=p, pp=pp, sl=sl, ws=ws, step=step):
                    w8(t, ws, w_mark[c])
                    if c == 0 and p == 0:
                        w8(t, "s_dve", ln_done)
                        if tg:
                            w8(t, "s_st", tg)
                    prev = step - 2
                    if prev >= 0:
                        w8(t, "s_act", act_gvd[prev // NP][prev % NP])
                    for kc in range(C):
                        st0 = (kc == 0); sp = (kc == C - 1)
                        t.matmul(ps_g[:, pp, :], wg_s[:, cb, kc, :], opnd[:, kc, sl], start=st0, stop=sp)
                        t.matmul(ps_v[:, pp, :], wv_s[:, cb, kc, :], opnd[:, kc, sl], start=st0, stop=sp)
                        t.matmul(ps_d[:, pp, :], wd_s[:, cb, kc, :], opnd[:, kc, sl], start=st0, stop=sp)
                    pe_gvd[c][p] = incl(t, "s_pe")
                sec("tensor", _)

                def _(s, c=c, cb=cb, p=p, pp=pp, sl=sl, step=step):
                    w8(s, "s_pe", pe_gvd[c][p])
                    if step >= 1:
                        prev = step - 1
                        w8(s, "s_dve", dve_axs[prev // NP][prev % NP])
                    if p == 0 and c >= 2:
                        w8(s, "s_dve", dve_y1[c - 2])
                    s.activation(out=sg_s[:, :], in_=ps_g[:, pp, :], func=AF.Sigmoid,
                                 bias=bg_t[:, c:c + 1], scale=1.0)
                    s.activation(out=tv_s[:, :], in_=ps_v[:, pp, :], func=AF.Tanh,
                                 bias=bv_t[:, c:c + 1], scale=1.0)
                    s.activation(out=a_s[:, cb, sl], in_=ps_d[:, pp, :], func=AF.Sigmoid,
                                 bias=bd_t[:, c:c + 1], scale=1.0)
                    act_gvd[c][p] = incl(s, "s_act")
                sec("scalar", _)

                def _(v, c=c, cb=cb, p=p, sl=sl):
                    w8(v, "s_act", act_gvd[c][p])
                    v.tensor_tensor(xs_s[:, cb, sl], sg_s[:, :], tv_s[:, :], OP.mult)
                    v.tensor_scalar(a_s[:, cb, sl], a_s[:, cb, sl], 0.998, 0.001,
                                    OP.mult, OP.add)
                    dve_axs[c][p] = incl(v, "s_dve")
                sec("vector", _)

            if c + 2 < C:
                stage_gru_w(c + 2)

            def _(v, c=c, cb=cb, gs=gs):
                w8(v, "s_dve", dve_axs[c][NP - 1])
                if c >= 2:
                    w8(v, gs, gath_mark[c - 2])
                v.drain()
                v.tensor_tensor_scan(h_s[:, :], a_s[:, cb, :], xs_s[:, cb, :],
                                     0.0, OP.mult, OP.add)
                v.drain()
                v.tensor_copy(pub_s[:, cb, 1:2], h_s[:, T - 1:T])
                dve_scan1[c] = incl(v, "s_dve")
            sec("vector", _)

            def _(s, c=c, cb=cb, gs=gs):
                w8(s, "s_dve", dve_axs[c][NP - 1])
                if c >= 2:
                    w8(s, gs, gath_mark[c - 2])
                for i in range(NP):
                    s.activation(out=dum_s[:, :], in_=a_s[:, cb, i * PIECE:(i + 1) * PIECE],
                                 func=AF.Ln, accum_out=ls_s[:, i:i + 1])
                s.drain()
                s.activation(out=ls_s[:, 4:5], in_=ls_s[:, 0:1], func=AF.Identity,
                             bias=ls_s[:, 1:2], scale=1.0)
                s.activation(out=ls_s[:, 5:6], in_=ls_s[:, 2:3], func=AF.Identity,
                             bias=ls_s[:, 3:4], scale=1.0)
                s.drain()
                s.activation(out=pub_s[:, cb, 0:1], in_=ls_s[:, 4:5], func=AF.Exp,
                             bias=ls_s[:, 5:6], scale=1.0)
                act_atot[c] = incl(s, "s_act")
            sec("scalar", _)

            def _(g, c=c, cb=cb, gs=gs):
                w8(g, "s_dve", dve_scan1[c])
                w8(g, "s_act", act_atot[c])
                m = dma(g, "s_pub", cc_sc_in[c][:, :], pub_s[:, cb, :])
                w8(g, "s_pub", m)
                if c >= 1:
                    w8(g, "s_cc", cc_mark[c - 1])
                cc = g.collective_compute(
                    "AllGather", OP.bypass,
                    replica_groups=[list(range(R))],
                    ins=[cc_sc_in[c][:, :]],
                    outs=[cc_sc_out[c][:, :]],
                )
                cc_mark[c] = inc(cc, "s_cc")
            sec("gpsimd", _)

            def _(sy, c=c, cb=cb, gs=gs):
                w8(sy, "s_cc", cc_mark[c])
                if c >= 2:
                    w8(sy, "s_dve", dve_y1[c - 2])
                gath_mark[c] = dma(
                    sy, gs, gath_s[:, cb, :].rearrange("p (j s) -> p j s", j=R),
                    cc_sc_out[c].rearrange("(j p) s -> p j s", p=128))
            sec("sync", _)

            if c >= 1:
                emit_tail(c - 1)
                flush()
        emit_tail(C - 1)
        flush()
        ph2["done"] = dve_y1[C - 1]
    _p2.close()
    _pw.close()

    gru_done = ph2["done"]
    tg = tap_xf("y1", gru_done)
    if BUILD_PHASES <= 2:
        return _finish(gru_done)

    def rms_pass(w_t, prev_dve, tag, st_gate=0):
        res = {}
        _pr = ExitStack()
        ps_st = _pr.enter_context(nc.psum_tensor(f"ps_st{tag}", [1, NP, PIECE], F32))
        ps_bb = _pr.enter_context(nc.psum_tensor(f"ps_bb{tag}", [128, 2, PIECE], F32))
        sq2 = _pr.enter_context(nc.sbuf_tensor(f"sq{tag}", [128, 2, PIECE], BF16))
        tp2 = _pr.enter_context(nc.sbuf_tensor(f"tp{tag}", [128, PIECE], F32))
        rows = _pr.enter_context(nc.sbuf_tensor(f"rows{tag}", [1, 5, PIECE], F32))
        if True:
            pe_st = [0] * NP
            pe_bb = [0] * NP
            act_r = [0] * NP
            dve_ap = [0] * NP
            pe_cs2 = {}
            act_cs2 = {}
            for p in range(NP):
                sl = slice(p * PIECE, (p + 1) * PIECE)
                pp = p % 2

                for c in range(C):
                    cs = p * C + c

                    def _(s, p=p, c=c, cs=cs, sl=sl):
                        if cs == 0:
                            w8(s, "s_dve", prev_dve)
                            if st_gate:
                                w8(s, "s_st", st_gate)
                        if cs >= 2:
                            w8(s, "s_pe", pe_cs2[cs - 2])
                        s.activation(out=sq2[:, cs % 2, :], in_=xf[:, c, sl],
                                     func=AF.Square)
                        act_cs2[cs] = incl(s, "s_act")
                    sec("scalar", _)

                    def _(t, p=p, c=c, cs=cs):
                        w8(t, "s_act", act_cs2[cs])
                        if cs == 0:
                            w8(t, "s_dve", mark_ones)
                        t.matmul(ps_st[0:1, p, :], ones_b[:, :], sq2[:, cs % 2, :],
                                 start=(c == 0), stop=(c == C - 1))
                        pe_cs2[cs] = incl(t, "s_pe")
                    sec("tensor", _)
                pe_st[p] = cnt["s_pe"]

                def _(s, p=p):
                    w8(s, "s_pe", pe_st[p])
                    if p >= 1:
                        w8(s, "s_pe", pe_bb[p - 1])
                    s.activation(out=rows[0:1, 4, :], in_=ps_st[0:1, p, :], func=AF.Ln,
                                 bias=eps2_t[0:1, :], scale=1.0 / D)
                    s.drain()
                    s.activation(out=rows[0:1, 3, :], in_=rows[0:1, 4, :], func=AF.Exp,
                                 scale=-0.5)
                    act_r[p] = incl(s, "s_act")
                sec("scalar", _)

                def _(t, p=p, pp=pp):
                    w8(t, "s_act", act_r[p])
                    if p >= 2:
                        w8(t, "s_dve", dve_ap[p - 2])
                    t.matmul(ps_bb[:, pp, :], ones_r[:, :], rows[0:1, 3, :],
                             start=True, stop=True)
                    pe_bb[p] = incl(t, "s_pe")
                sec("tensor", _)

                def _(v, p=p, pp=pp, sl=sl):
                    w8(v, "s_pe", pe_bb[p])
                    for c in range(C):
                        v.tensor_tensor(tp2[:, :], xf[:, c, sl], ps_bb[:, pp, :], OP.mult)
                        v.tensor_scalar(opnd[:, c, sl], tp2[:, :], w_t[:, c:c + 1],
                                        None, OP.mult)
                    dve_ap[p] = incl(v, "s_dve")
                sec("vector", _)
            res["done"] = dve_ap[NP - 1]
        flush()
        _pr.close()
        return res["done"]

    ya_done = rms_pass(wattn_t, gru_done, "a", st_gate=tg)
    tg = tap_opnd("ya", ya_done)
    if BUILD_PHASES <= 3:
        return _finish(ya_done)

    ph4 = {}
    _p4 = ExitStack()
    ps_qkv = _p4.enter_context(nc.psum_tensor("ps_qkv", [128, 2, PIECE], F32))
    ps_sc = _p4.enter_context(nc.psum_tensor("ps_sc", [128, 4, PIECE], F32))
    ps_pt = _p4.enter_context(nc.psum_tensor("ps_pt", [128, 2, 1024], BF16))
    wq_s = _p4.enter_context(nc.sbuf_tensor("wq_s", [128, 2, C, 128], BF16))
    wk_s = _p4.enter_context(nc.sbuf_tensor("wk_s", [128, 2, C, 128], BF16))
    wv2_s = _p4.enter_context(nc.sbuf_tensor("wv2_s", [128, 2, C, 128], BF16))
    qp_sb = _p4.enter_context(nc.sbuf_tensor("qp_sb", [128, 2, 2, T], BF16))
    k_sb = _p4.enter_context(nc.sbuf_tensor("k_sb", [128, 2, T], BF16))
    v_st = _p4.enter_context(nc.sbuf_tensor("v_st", [128, 2, PIECE], BF16))
    vt_sb = _p4.enter_context(nc.sbuf_tensor("vt_sb", [128, 2, NCH, 128], BF16))
    prb_sb = _p4.enter_context(nc.sbuf_tensor("prb_sb", [128, 4, 2, 256], BF16))
    prt_sb = _p4.enter_context(nc.sbuf_tensor("prt_sb", [128, 2, 4, 128], BF16))
    smx_sb = _p4.enter_context(nc.sbuf_tensor("smx_sb", [128, 4, 6], F32))
    gkv_sb = _p4.enter_context(nc.sbuf_tensor("gkv_sb", [128, 2, 1024], BF16))
    ret_qp = _p4.enter_context(nc.sbuf_tensor("ret_qp", [128, PAIRS, 2, 128], BF16))
    ret_k = _p4.enter_context(nc.sbuf_tensor("ret_k", [128, PAIRS, 256], BF16))
    ret_vt = _p4.enter_context(nc.sbuf_tensor("ret_vt", [128, PAIRS, 2, 128], BF16))
    if True:
        wsem = ["s_w0", "s_w1"]
        from collections import defaultdict as _dd
        G_pe_sc = _dd(int); G_dve_mr = _dd(int); G_act_exp = _dd(int); G_dve_rc = _dd(int)
        G_pool = _dd(int); G_pe_pt = _dd(int); G_dve_ptc = _dd(int); G_pe_av = _dd(int); G_dve_avc = _dd(int)
        gcnt = [0]
        act_qkv = {}
        pe_qkv = {}
        vt_dma = {}
        ret_act = [0] * PAIRS
        pub_v_mark = [0] * PAIRS
        pair_last_pe = [0] * PAIRS
        pe_wo = {}

        def group_stages(hp, ch, pb, waits):
            g = gcnt[0]; gcnt[0] += 1
            g2 = g % 2
            g4 = g % 4
            mask = mfirst_t if ch == 0 else mstd_t
            stages = []

            def _(t):
                for sn, vv in waits:
                    w8(t, sn, vv)
                if g >= 4:
                    w8(t, "s_act", G_act_exp[g - 4])
                    w8(t, "s_dve", G_dve_avc[g - 4])
                t.matmul(ps_sc[:, g4, 0:512], eyeb_t[:, :], mask[:, :],
                         start=True, stop=False, skip_group_check=True)
                for h2 in range(2):
                    if ch == 0:
                        t.matmul(ps_sc[:, g4, 256 * h2:256 * h2 + 256],
                                 ret_qp[:, hp, h2, :], ret_k[:, hp, :],
                                 start=False, stop=True, skip_group_check=True)
                    else:
                        t.matmul(ps_sc[:, g4, 256 * h2:256 * h2 + 256],
                                 qp_sb[:, pb, h2, 128 * ch:128 * (ch + 1)],
                                 k_sb[:, pb, 128 * (ch - 1):128 * (ch - 1) + 256],
                                 start=False, stop=True, skip_group_check=True)
                G_pe_sc[g] = incl(t, "s_pe")
            stages.append(("tensor", _))

            def _(v):
                w8(v, "s_pe", G_pe_sc[g])
                if g >= 4:
                    w8(v, "s_act", G_act_exp[g - 4])
                v.reduce_max(smx_sb[:, g4, 0:2],
                             ps_sc[:, g4, 0:512].rearrange("p (u k) -> p u k", u=2),
                             axis=AX.X, negate=True)
                G_dve_mr[g] = incl(v, "s_dve")
            stages.append(("vector", _))

            def _(s):
                w8(s, "s_dve", G_dve_mr[g])
                if g >= 4:
                    w8(s, "s_pool", G_pool[g - 4])
                for h2 in range(2):
                    s.activation(out=prb_sb[:, g4, h2, :],
                                 in_=ps_sc[:, g4, 256 * h2:256 * h2 + 256],
                                 func=AF.Exp, bias=smx_sb[:, g4, h2:h2 + 1], scale=1.0,
                                 accum_out=smx_sb[:, g4, 2 + h2:3 + h2])
                G_act_exp[g] = incl(s, "s_act")
            stages.append(("scalar", _))

            def _(v):
                w8(v, "s_act", G_act_exp[g])
                v.reciprocal(smx_sb[:, g4, 4:6], smx_sb[:, g4, 2:4])
                G_dve_rc[g] = incl(v, "s_dve")
            stages.append(("vector", _))

            def _(gp):
                w8(gp, "s_dve", G_dve_rc[g])
                for h2 in range(2):
                    gp.tensor_scalar(prb_sb[:, g4, h2, :], prb_sb[:, g4, h2, :],
                                     smx_sb[:, g4, 4 + h2:5 + h2], 0.0, OP.mult, OP.add)
                G_pool[g] = incl(gp, "s_pool")
            stages.append(("gpsimd", _))

            def _(t):
                w8(t, "s_pool", G_pool[g])
                if g >= 2:
                    w8(t, "s_dve", G_dve_ptc[g - 2])
                for j in range(4):
                    t.transpose(ps_pt[:, g2, 128 * j:128 * (j + 1)],
                                prb_sb[:, g4, j // 2, 128 * (j % 2):128 * (j % 2) + 128],
                                eyeb_t[:, :])
                G_pe_pt[g] = incl(t, "s_pe")
            stages.append(("tensor", _))

            def _(v):
                w8(v, "s_pe", G_pe_pt[g])
                if g >= 2:
                    w8(v, "s_pe", G_pe_av[g - 2])
                v.tensor_copy(prt_sb[:, g2, :, :], ps_pt[:, g2, 0:512])
                G_dve_ptc[g] = incl(v, "s_dve")
            stages.append(("vector", _))

            def _(t):
                w8(t, "s_dve", G_dve_ptc[g])
                if ch > 0:
                    w8(t, "s_q", vt_dma[(hp, ch)])
                for h2 in range(2):
                    hb = 64 * h2
                    av = ps_sc[0:64, g4, 128 * h2:128 * h2 + 128]
                    if ch == 0:
                        t.matmul(av, ret_vt[:, hp, 0, hb:hb + 64],
                                 prt_sb[:, g2, 2 * h2, :], start=True, stop=False)
                        t.matmul(av, ret_vt[:, hp, 1, hb:hb + 64],
                                 prt_sb[:, g2, 2 * h2 + 1, :], start=False, stop=True)
                    else:
                        t.matmul(av, vt_sb[:, pb, ch - 1, hb:hb + 64],
                                 prt_sb[:, g2, 2 * h2, :], start=True, stop=False)
                        t.matmul(av, vt_sb[:, pb, ch, hb:hb + 64],
                                 prt_sb[:, g2, 2 * h2 + 1, :], start=False, stop=True)
                G_pe_av[g] = incl(t, "s_pe")
            stages.append(("tensor", _))

            def _(v):
                w8(v, "s_pe", G_pe_av[g])
                for h2 in range(2):
                    hb = 64 * h2
                    v.tensor_copy(sbf[hb:hb + 64, hp, 128 * ch:128 * (ch + 1)],
                                  ps_sc[0:64, g4, 128 * h2:128 * h2 + 128])
                G_dve_avc[g] = incl(v, "s_dve")
            stages.append(("vector", _))
            return stages[:G4LIM]

        _pend = []

        def _emit_batch(batch):
            if not batch:
                return
            ns = max(len(sl) for sl in batch)
            for si in range(min(5, ns)):
                for sl in batch:
                    if si < len(sl):
                        sec(*sl[si])
            for k in range(0, len(batch), 2):
                sub = batch[k:k + 2]
                for si in range(5, ns):
                    for sl in sub:
                        if si < len(sl):
                            sec(*sl[si])

        def emit_group(hp, ch, pb, waits):
            _pend.append(group_stages(hp, ch, pb, waits))
            if len(_pend) == 4:
                _emit_batch(_pend)
                _pend.clear()

        def flush_groups():
            _emit_batch(_pend)
            _pend.clear()

        def _(v):
            v.memset(qp_sb[64:128, :, 0, :], 0.0)
            v.memset(qp_sb[0:64, :, 1, :], 0.0)
            v.memset(ret_k[:, :, 0:128], 0.0)
            v.memset(ret_vt[:, :, 0, :], 0.0)
            incl(v, "s_dve")
        sec("vector", _)
        m_pad = cnt["s_dve"]

        qkv_wm = [0] * (PAIRS + 2)

        def stage_qkv_w(hp):
            hs = hp % 2
            ws = wsem[hs]

            def _(g, hp=hp, hs=hs):
                if hp >= 2:
                    w8(g, "s_pe", pe_qkv[(hp - 2, 1, NP - 1)])
                else:
                    w8(g, "s_dve", ya_done)
                if hp == 0 and tg:
                    w8(g, "s_st", tg)
                for mi, base in ((0, 0), (1, D), (2, 2 * D)):
                    dma(g, ws, [wq_s, wk_s, wv2_s][mi][:, hs, :, :],
                        wqkvt[:, base + 128 * hp:base + 128 * (hp + 1)]
                        .rearrange("(k p) m -> p k m", p=128))
            sec("gpsimd", _)
            qkv_wm[hp] = cnt[ws]

        stage_qkv_w(0)
        stage_qkv_w(1)
        for hp in range(PAIRS):
            pb = hp % 2
            hs = hp % 2
            ws = wsem[hs]
            wm = qkv_wm[hp]

            MI_ORDER = (2, 0, 1)
            for idx, mi in enumerate(MI_ORDER):
                for p in range(NP):
                    sl = slice(p * PIECE, (p + 1) * PIECE)
                    qs = hp * 12 + idx * NP + p
                    qb = qs % 2

                    def _(t, hp=hp, pb=pb, hs=hs, mi=mi, p=p, sl=sl, qs=qs, qb=qb, ws=ws, wm=wm):
                        w8(t, ws, wm)
                        if hp == 0 and qs == 0:
                            w8(t, "s_dve", ya_done)
                        prev = qs - 2
                        if prev >= 0:
                            hpp, rem = divmod(prev, 12)
                            w8(t, "s_act",
                               act_qkv[(hpp, MI_ORDER[rem // NP], rem % NP)])
                        wsrc = [wq_s, wk_s, wv2_s][mi]
                        for kc in range(C):
                            t.matmul(ps_qkv[:, qb, :], wsrc[:, hs, kc, :], opnd[:, kc, sl],
                                     start=(kc == 0), stop=(kc == C - 1))
                        pe_qkv[(hp, mi, p)] = incl(t, "s_pe")
                    sec("tensor", _)

                    def _(s, hp=hp, pb=pb, mi=mi, p=p, sl=sl, qs=qs, qb=qb):
                        w8(s, "s_pe", pe_qkv[(hp, mi, p)])
                        if hp >= 2 and mi == 0 and p == 0:
                            w8(s, "s_pe", pair_last_pe[hp - 2])
                        if mi == 2:
                            if p >= 2:
                                w8(s, "s_q", vt_dma[(hp, 4 * (p - 2) + 3)])
                            elif hp >= 1 and p == 1:
                                w8(s, "s_pub", pub_v_mark[hp - 1])
                            elif hp >= 1:
                                w8(s, "s_q", vt_dma[(hp - 1, 4 * (NP - 2 + p) + 3)])
                            s.activation(out=v_st[:, p % 2, :], in_=ps_qkv[:, qb, :],
                                         func=AF.Copy)
                        elif mi == 0:
                            s.activation(out=qp_sb[0:64, pb, 0, sl],
                                         in_=ps_qkv[0:64, qb, :],
                                         func=AF.Copy, scale=0.125)
                            s.activation(out=qp_sb[64:128, pb, 1, sl],
                                         in_=ps_qkv[64:128, qb, :],
                                         func=AF.Copy, scale=0.125)
                        else:
                            s.activation(out=k_sb[:, pb, sl], in_=ps_qkv[:, qb, :],
                                         func=AF.Copy)
                        act_qkv[(hp, mi, p)] = incl(s, "s_act")
                    sec("scalar", _)

                    if mi == 2:
                        def _(sy, hp=hp, pb=pb, p=p):
                            w8(sy, "s_act", act_qkv[(hp, 2, p)])
                            if hp >= 2 and p == 0:
                                w8(sy, "s_pe", pair_last_pe[hp - 2])
                            for c4 in range(4):
                                ch = 4 * p + c4
                                _i = sy.dma_start_transpose(
                                    vt_sb[:, pb, ch, :],
                                    v_st[:, p % 2, 128 * c4:128 * (c4 + 1)])
                                vt_dma[(hp, ch)] = inc(_i, "s_q", 16)
                        sec("sync", _)

                        if p == NP - 1:
                            def _(g, hp=hp, pb=pb):
                                w8(g, "s_q", vt_dma[(hp, NCH - 1)])
                                pub_v_mark[hp] = dma(
                                    g, "s_pub",
                                    cc_kv_in[:, 1024 + 128 * hp:1024 + 128 * (hp + 1)],
                                    vt_sb[:, pb, NCH - 1, :])
                            sec("gpsimd", _)

            def _(s, hp=hp, pb=pb):
                w8(s, "s_act", act_qkv[(hp, 1, NP - 1)])
                w8(s, "s_dve", m_pad)
                w8(s, "s_q", vt_dma[(hp, 0)])
                s.copy(ret_qp[:, hp, 0, :], qp_sb[:, pb, 0, 0:128])
                s.copy(ret_qp[:, hp, 1, :], qp_sb[:, pb, 1, 0:128])
                s.copy(ret_k[:, hp, 128:256], k_sb[:, pb, 0:128])
                s.copy(ret_vt[:, hp, 1, :], vt_sb[:, pb, 0, :])
                ret_act[hp] = incl(s, "s_act")
            sec("scalar", _)

            def _(g, hp=hp, pb=pb):
                w8(g, "s_act", act_qkv[(hp, 1, NP - 1)])
                dma(g, "s_pub", cc_kv_in[:, 128 * hp:128 * (hp + 1)],
                    k_sb[:, pb, T - 128:T])
            sec("gpsimd", _)

            if hp + 2 < PAIRS:
                stage_qkv_w(hp + 2)

            ready = [("s_act", max(act_qkv[(hp, mi, NP - 1)] for mi in range(3))),
                     ("s_dve", m_pad)]
            if BUILD_PHASES != 34:
                for ch in range(1, NCH):
                    emit_group(hp, ch, pb, ready)
                flush_groups()
            pair_last_pe[hp] = cnt["s_pe"]
            flush()

        if BUILD_PHASES in (34, 35):
            _done35 = max(cnt["s_dve"], 1)
            _p4.close()
            return _finish(_done35)

        std_done = cnt["s_dve"]

        wo_evac = {}
        wo_last_pe = {}
        wo_wm = {}
        wo_seq = [0]
        wo_tiles = [wq_s, wk_s, wv2_s]

        def stage_wo_load(li, m):
            tile = wo_tiles[li % 3]
            slot = (li // 3) % 2
            ws = wsem[li % 2]

            def _(g, li=li, m=m, tile=tile, slot=slot, ws=ws):
                if li >= 6:
                    w8(g, "s_pe", wo_last_pe[li - 6])
                else:
                    w8(g, "s_pe", pe_qkv[(PAIRS - 2 + slot, 1, NP - 1)])
                dma(g, ws, tile[:, slot, :, :],
                    wot[:, 128 * m:128 * (m + 1)].rearrange("(k p) m -> p k m", p=128))
            sec("gpsimd", _)
            wo_wm[li] = cnt[ws]

        def stage_wo_mms(li, m, plist, dep_mark):
            tile = wo_tiles[li % 3]
            slot = (li // 3) % 2
            ws = wsem[li % 2]

            for p in plist:
                sl = slice(p * PIECE, (p + 1) * PIECE)
                qs = wo_seq[0]; wo_seq[0] += 1
                qb = qs % 2

                def _(t, li=li, m=m, tile=tile, slot=slot, p=p, sl=sl, qs=qs,
                      qb=qb, ws=ws, dep_mark=dep_mark):
                    w8(t, ws, wo_wm[li])
                    w8(t, "s_dve", dep_mark)
                    if qs >= 2:
                        w8(t, "s_dve", wo_evac[qs - 2])
                    for kc in range(C):
                        t.matmul(ps_qkv[:, qb, :], tile[:, slot, kc, :],
                                 sbf[:, kc, sl],
                                 start=(kc == 0), stop=(kc == C - 1))
                    pe_wo[qs] = incl(t, "s_pe")
                    wo_last_pe[li] = pe_wo[qs]
                sec("tensor", _)

                def _(v, m=m, p=p, sl=sl, qs=qs, qb=qb):
                    w8(v, "s_pe", pe_wo[qs])
                    v.scalar_tensor_tensor(xf[:, m, sl], ps_qkv[:, qb, :], 1.0,
                                           xf[:, m, sl], OP.mult, OP.add)
                    wo_evac[qs] = incl(v, "s_dve")
                sec("vector", _)

        for li in range(6):
            stage_wo_load(li, li)

        _mq = cnt["s_q"]; _mc = cnt["s_cc"]; _mp = cnt["s_pub"]
        def _(g, _mq=_mq, _mc=_mc, _mp=_mp):
            w8(g, "s_q", _mq)
            w8(g, "s_pub", _mp)
            w8(g, "s_cc", _mc)
            cc = g.collective_compute(
                "AllGather", OP.bypass,
                replica_groups=[list(range(R))],
                ins=[cc_kv_in[:, :]],
                outs=[cc_kv_out[:, :]],
            )
            inc(cc, "s_cc")
        sec("gpsimd", _)
        m_kcc = cnt["s_cc"]

        gsel = ["s_g0", "s_g1"]
        sel_dma = {}
        dve_sel = {}
        for j in range(R):
            for hh in range(2):
                i = 2 * j + hh
                ib = i % 2

                def _(sy, j=j, hh=hh, i=i, ib=ib):
                    if i < 2:
                        w8(sy, "s_cc", m_kcc)
                    if i >= 2:
                        w8(sy, "s_dve", dve_sel[i - 2])
                    sel_dma[i] = dma(sy, gsel[ib], gkv_sb[:, ib, :],
                                     cc_kv_out[128 * j:128 * (j + 1),
                                               1024 * hh:1024 * hh + 1024])
                sec("sync", _)

                _ma = cnt["s_act"]
                def _(v, j=j, hh=hh, i=i, ib=ib, _ma=_ma):
                    w8(v, gsel[ib], sel_dma[i])
                    if i < 2:
                        w8(v, "s_act", _ma)
                    dst = ret_k[:, :, 0:128] if hh == 0 else ret_vt[:, :, 0, :]
                    if i < 2:
                        w8(v, "s_dve", m_pad)
                    v.scalar_tensor_tensor(
                        dst, gkv_sb[:, ib, :].rearrange("p (a b) -> p a b", a=PAIRS),
                        selh_t[:, j:j + 1], dst,
                        OP.mult, OP.add)
                    v.drain()
                    dve_sel[i] = incl(v, "s_dve")
                sec("vector", _)

        m_sel = cnt["s_dve"]
        flush()
        if BUILD_PHASES == 36:
            _p4.close()
            return _finish(m_sel)

        for m in range(C):
            if m >= 6:
                stage_wo_load(m, m)
            stage_wo_mms(m, m, [1, 2, 3], std_done)

        for hp in range(PAIRS):
            emit_group(hp, 0, hp % 2,
                       [("s_dve", m_sel), ("s_act", ret_act[hp])])
        flush_groups()

        if BUILD_PHASES == 37:
            _done37 = cnt["s_dve"]
            _p4.close()
            return _finish(_done37)

        attn_units_done = cnt["s_dve"]
        flush()

        for m in range(C):
            stage_wo_load(C + m, m)
            stage_wo_mms(C + m, m, [0], attn_units_done)
        ph4["done"] = wo_evac[wo_seq[0] - 1]
        flush()
    _p4.close()

    y2_done = ph4["done"]
    tg = tap_xf("y2", y2_done)
    if BUILD_PHASES <= 4:
        return _finish(y2_done)

    yf_done = rms_pass(wffn_t, y2_done, "f", st_gate=tg)
    tg = tap_opnd("yf", yf_done)

    ph5 = {}
    _p5 = ExitStack()
    ps_ga = _p5.enter_context(nc.psum_tensor("ps_ga", [128, 2, PIECE], F32))
    ps_up = _p5.enter_context(nc.psum_tensor("ps_up", [128, 2, PIECE], F32))
    ps_o = _p5.enter_context(nc.psum_tensor("ps_o", [128, 2, PIECE], F32))
    wga_s = _p5.enter_context(nc.sbuf_tensor("wga_s", [128, 2, C, PIECE], BF16))
    wup_s = _p5.enter_context(nc.sbuf_tensor("wup_s", [128, 2, C, PIECE], BF16))
    wou_s = _p5.enter_context(nc.sbuf_tensor("wou_s", [128, 2, 4, D], BF16))
    sgf_s = _p5.enter_context(nc.sbuf_tensor("sgf_s", [128, 2, PIECE], F32))
    if True:

        wsem = ["s_w0", "s_w1"]
        pe_h = {}
        dve_h = {}
        pe_ow = {}
        dve_ow = {}
        for j in range(NJ):
            jb = j % 2
            ws = wsem[jb]

            def _(g, j=j, jb=jb, ws=ws):
                if j >= 2:
                    w8(g, "s_pe", pe_ow[(j - 2, NP - 1, C - 1)])
                else:
                    w8(g, "s_dve", yf_done)
                if j == 0 and tg:
                    w8(g, "s_st", tg)
                dma(g, ws, wga_s[:, jb, :, :],
                    wgatet[:, PIECE * j:PIECE * (j + 1)].rearrange("(k p) m -> p k m", p=128))
                dma(g, ws, wup_s[:, jb, :, :],
                    wupt[:, PIECE * j:PIECE * (j + 1)].rearrange("(k p) m -> p k m", p=128))
                dma(g, ws, wou_s[:, jb, :, :],
                    woutt[PIECE * j:PIECE * (j + 1), :].rearrange("(k p) m -> p k m", p=128))
            sec("gpsimd", _)
            wm = cnt[ws]

            for p in range(NP):
                sl = slice(p * PIECE, (p + 1) * PIECE)
                for jm in range(4):
                    qs = (j * NP + p) * 4 + jm
                    qb = qs % 2

                    def _(t, j=j, jb=jb, p=p, sl=sl, jm=jm, qs=qs, qb=qb, ws=ws, wm=wm):
                        w8(t, ws, wm)
                        if j == 0 and p == 0 and jm == 0:
                            w8(t, "s_dve", yf_done)
                        if qs >= 2:
                            w8(t, "s_dve", dve_h[qs - 2])
                        for kc in range(C):
                            t.matmul(ps_ga[:, qb, :], wga_s[:, jb, kc, 128 * jm:128 * (jm + 1)],
                                     opnd[:, kc, sl], start=(kc == 0), stop=(kc == C - 1))
                        for kc in range(C):
                            t.matmul(ps_up[:, qb, :], wup_s[:, jb, kc, 128 * jm:128 * (jm + 1)],
                                     opnd[:, kc, sl], start=(kc == 0), stop=(kc == C - 1))
                        pe_h[qs] = incl(t, "s_pe")
                    sec("tensor", _)

                    def _(s, qs=qs, qb=qb):
                        w8(s, "s_pe", pe_h[qs])
                        if qs >= 2:
                            w8(s, "s_dve", dve_h[qs - 2])
                        s.activation(out=sgf_s[:, qb, :], in_=ps_ga[:, qb, :], func=AF.Silu)
                        incl(s, "s_act")
                    sec("scalar", _)
                    m_si = cnt["s_act"]

                    def _(v, j=j, jb=jb, p=p, sl=sl, jm=jm, qs=qs, qb=qb, m_si=m_si):
                        w8(v, "s_act", m_si)
                        v.tensor_tensor(sbf[:, 4 * jb + jm, sl], sgf_s[:, qb, :],
                                        ps_up[:, qb, :], OP.mult)
                        dve_h[qs] = incl(v, "s_dve")
                    sec("vector", _)

            for p in range(NP):
                sl = slice(p * PIECE, (p + 1) * PIECE)
                for m in range(C):
                    qs = p * C + m
                    qb = qs % 2

                    def _(t, j=j, jb=jb, p=p, sl=sl, m=m, qs=qs, qb=qb):
                        w8(t, "s_dve", dve_h[(j * NP + p) * 4 + 3])
                        if qs >= 2:
                            w8(t, "s_dve", dve_ow[(j, (qs - 2) // C, (qs - 2) % C)])
                        elif j >= 1:
                            pv = NP * C + qs - 2
                            w8(t, "s_dve", dve_ow[(j - 1, pv // C, pv % C)])
                        for kc in range(4):
                            t.matmul(ps_o[:, qb, :], wou_s[:, jb, kc, 128 * m:128 * (m + 1)],
                                     sbf[:, 4 * jb + kc, sl], start=(kc == 0), stop=(kc == 3))
                        pe_ow[(j, p, m)] = incl(t, "s_pe")
                    sec("tensor", _)

                    def _(v, j=j, p=p, sl=sl, m=m, qs=qs, qb=qb):
                        w8(v, "s_pe", pe_ow[(j, p, m)])
                        v.scalar_tensor_tensor(xf[:, m, sl], ps_o[:, qb, :], 1.0,
                                               xf[:, m, sl], OP.mult, OP.add)
                        dve_ow[(j, p, m)] = incl(v, "s_dve")
                    sec("vector", _)

        ph5["done"] = dve_ow[(NJ - 1, NP - 1, C - 1)]
        flush()
    _p5.close()

    y3_done = ph5["done"]
    tg = tap_xf("y3", y3_done)
    if BUILD_PHASES <= 5:
        return _finish(y3_done)

    _p6 = ExitStack()
    lgw_sb = _p6.enter_context(nc.sbuf_tensor("lgw_sb", [NE, T], F32))
    ltok = _p6.enter_context(nc.sbuf_tensor("ltok", [128, 16, NE], F32))
    gm1 = _p6.enter_context(nc.sbuf_tensor("gm1", [128, 16], F32))
    gm2 = _p6.enter_context(nc.sbuf_tensor("gm2", [128, 16], F32))
    ge = _p6.enter_context(nc.sbuf_tensor("ge", [128, 16, NE], F32))
    gk1 = _p6.enter_context(nc.sbuf_tensor("gk1", [128, 16, NE], F32))
    gk2 = _p6.enter_context(nc.sbuf_tensor("gk2", [128, 16, NE], F32))
    gsc = _p6.enter_context(nc.sbuf_tensor("gsc", [128, 16, NE], F32))
    we0 = _p6.enter_context(nc.sbuf_tensor("we0", [1, 2, T], BF16))
    bem8 = _p6.enter_context(nc.sbuf_tensor("bem8", [NE, D], BF16))
    wgm_s = _p6.enter_context(nc.sbuf_tensor("wgm_s", [128, C, NE], BF16))
    if True:

        _p6a = ExitStack()
        xrl = _p6a.enter_context(nc.sbuf_tensor("xrl", [128, C, PIECE], F32))
        rows = _p6a.enter_context(nc.sbuf_tensor("rows6", [1, 5, PIECE], F32))
        cast2 = _p6a.enter_context(nc.sbuf_tensor("cast2", [128, 2, PIECE], BF16))
        sq2m = _p6a.enter_context(nc.sbuf_tensor("sq2m", [128, 2, PIECE], BF16))
        tmp2 = _p6a.enter_context(nc.sbuf_tensor("tmp2", [128, PIECE], F32))
        ps_mu2 = _p6a.enter_context(nc.psum_tensor("ps_mu2", [1, 2, PIECE], F32))
        ps_ms2 = _p6a.enter_context(nc.psum_tensor("ps_ms2", [1, 2, PIECE], F32))
        ps_c1 = _p6a.enter_context(nc.psum_tensor("ps_c1", [128, PIECE], F32))
        ps_c2 = _p6a.enter_context(nc.psum_tensor("ps_c2", [128, PIECE], F32))
        if True:

            x_mark = [0] * NP
            pe_st2 = [0] * NP
            pe_cs3 = {}
            act_cs3 = {}
            dve_var2 = [0] * NP
            pe_cb2 = [0] * NP
            dve_ap2 = [0] * NP

            for p in range(NP):
                sl = slice(p * PIECE, (p + 1) * PIECE)
                pp = p % 2
                xs_ = ["s_g0", "s_g1"][pp]

                def _(sy, p=p, sl=sl, xs_=xs_):
                    if p >= 1:
                        w8(sy, "s_dve", dve_ap2[p - 1])
                    else:
                        w8(sy, "s_dve", y3_done)
                    if p == 0 and tg:
                        w8(sy, "s_st", tg)
                    x_mark[p] = dma(sy, xs_, xrl[:, :, :],
                                    x_f[:, sl].rearrange("(c p) t -> p c t", p=128))
                sec("sync", _)

                for c in range(C):
                    cs = p * C + c

                    def _(s, p=p, c=c, cs=cs, xs_=xs_):
                        if c == 0:
                            w8(s, xs_, x_mark[p])
                        if cs >= 2:
                            w8(s, "s_pe", pe_cs3[cs - 2])
                        s.activation(out=cast2[:, cs % 2, :], in_=xrl[:, c, :], func=AF.Copy)
                        s.activation(out=sq2m[:, cs % 2, :], in_=xrl[:, c, :], func=AF.Square)
                        act_cs3[cs] = incl(s, "s_act")
                    sec("scalar", _)

                    def _(t, p=p, c=c, cs=cs, pp=pp):
                        w8(t, "s_act", act_cs3[cs])
                        if cs == 0:
                            w8(t, "s_dve", y3_done)
                        if c == 0 and p >= 2:
                            w8(t, "s_dve", dve_var2[p - 2])
                        t.matmul(ps_mu2[0:1, pp, :], ones_b[:, :], cast2[:, cs % 2, :],
                                 start=(c == 0), stop=(c == C - 1))
                        t.matmul(ps_ms2[0:1, pp, :], ones_b[:, :], sq2m[:, cs % 2, :],
                                 start=(c == 0), stop=(c == C - 1))
                        pe_cs3[cs] = incl(t, "s_pe")
                    sec("tensor", _)
                pe_st2[p] = cnt["s_pe"]

                def _(v, p=p, pp=pp):
                    w8(v, "s_pe", pe_st2[p])
                    if p >= 1:
                        w8(v, "s_pe", pe_cb2[p - 1])
                    v.tensor_scalar(rows[0:1, 0, :], ps_mu2[0:1, pp, :], 1.0 / D, None, OP.mult)
                    v.tensor_scalar(rows[0:1, 4, :], ps_ms2[0:1, pp, :], 1.0 / D, None, OP.mult)
                    v.tensor_tensor(rows[0:1, 1, :], rows[0:1, 0, :], rows[0:1, 0, :], OP.mult)
                    v.tensor_tensor(rows[0:1, 1, :], rows[0:1, 4, :], rows[0:1, 1, :], OP.subtract)
                    dve_var2[p] = incl(v, "s_dve")
                sec("vector", _)

                def _(s, p=p):
                    w8(s, "s_dve", dve_var2[p])
                    s.activation(out=rows[0:1, 4, :], in_=rows[0:1, 1, :], func=AF.Ln, bias=eps1_t[0:1, :])
                    s.drain()
                    s.activation(out=rows[0:1, 2, :], in_=rows[0:1, 4, :], func=AF.Exp, scale=-0.5)
                    incl(s, "s_act")
                sec("scalar", _)
                m_r2b = cnt["s_act"]

                def _(v, m_r2b=m_r2b):
                    w8(v, "s_act", m_r2b)
                    v.tensor_tensor(rows[0:1, 0, :], rows[0:1, 0, :], rows[0:1, 2, :], OP.mult)
                    incl(v, "s_dve")
                sec("vector", _)
                m_mr = cnt["s_dve"]

                def _(t, p=p, m_mr=m_mr):
                    w8(t, "s_dve", m_mr)
                    if p >= 1:
                        w8(t, "s_dve", dve_ap2[p - 1])
                    t.matmul(ps_c1[:, :], ones_r[:, :], rows[0:1, 2, :], start=True, stop=True)
                    t.matmul(ps_c2[:, :], ones_r[:, :], rows[0:1, 0, :], start=True, stop=True)
                    pe_cb2[p] = incl(t, "s_pe")
                sec("tensor", _)

                def _(v, p=p, sl=sl):
                    w8(v, "s_pe", pe_cb2[p])
                    for c in range(C):
                        v.tensor_tensor(tmp2[:, :], xrl[:, c, :], ps_c1[:, :], OP.mult)
                        v.tensor_tensor(opnd[:, c, sl], tmp2[:, :], ps_c2[:, :], OP.subtract)
                        v.tensor_tensor(xf[:, c, sl], xf[:, c, sl], xrl[:, c, :], OP.add)
                    dve_ap2[p] = incl(v, "s_dve")
                sec("vector", _)

            xn2_done = dve_ap2[NP - 1]
            flush()
        _p6a.close()

        _p6b = ExitStack()
        ps_big = _p6b.enter_context(nc.psum_tensor("ps_big", [NE, T], F32))
        ps_lt = _p6b.enter_context(nc.psum_tensor("ps_lt", [128, 16, NE], F32))
        if True:

            def _(g):
                w8(g, "s_dve", y3_done)
                dma(g, "s_q", wgm_s[:, :, :], wgm.rearrange("(k p) m -> p k m", p=128))
            sec("gpsimd", _)
            m_wgm = cnt["s_q"]

            pe_lg = [0] * NP
            for p in range(NP):
                def _(t, p=p):
                    w8(t, "s_q", m_wgm)
                    w8(t, "s_dve", xn2_done)
                    for kc in range(C):
                        t.matmul(ps_big[0:NE, p * PIECE:(p + 1) * PIECE], wgm_s[:, kc, :],
                                 opnd[:, kc, p * PIECE:(p + 1) * PIECE],
                                 start=(kc == 0), stop=(kc == C - 1))
                    pe_lg[p] = incl(t, "s_pe")
                sec("tensor", _)

            def _(s):
                for p in range(NP):
                    w8(s, "s_pe", pe_lg[p])
                    s.activation(out=lgw_sb[:, p * PIECE:(p + 1) * PIECE],
                                 in_=ps_big[0:NE, p * PIECE:(p + 1) * PIECE],
                                 func=AF.Identity, bias=bgm_t[:, :], scale=1.0)
                incl(s, "s_act")
            sec("scalar", _)
            m_lg = cnt["s_act"]

            for g16 in range(16):
                def _(t, g16=g16):
                    w8(t, "s_act", m_lg)
                    t.transpose(ps_lt[:, g16, :], lgw_sb[:, 128 * g16:128 * (g16 + 1)],
                                eyef_t[0:NE, 0:NE])
                    incl(t, "s_pe")
                sec("tensor", _)
            m_ltp = cnt["s_pe"]

            def _(s):
                w8(s, "s_pe", m_ltp)
                s.copy(ltok[:, :, :], ps_lt[:, :, :])
                incl(s, "s_act")
            sec("scalar", _)
            m_ltc = cnt["s_act"]

            def _(v):
                w8(v, "s_act", m_ltc)
                v.reduce_max(gm1[:, :], ltok[:, :, :], axis=AX.X)
                v.drain()
                for e in range(NE):
                    v.tensor_tensor(ge[:, :, e], ltok[:, :, e], gm1[:, :], OP.subtract)
                v.drain()
                incl(v, "s_dve")
            sec("vector", _)
            m_gs = cnt["s_dve"]

            def _(s):
                w8(s, "s_dve", m_gs)
                s.activation(out=ge[:, :, :], in_=ge[:, :, :], func=AF.Exp)
                incl(s, "s_act")
            sec("scalar", _)
            m_ge = cnt["s_act"]

            def _(v):
                w8(v, "s_act", m_ge)
                v.reduce_max(gm1[:, :], ge[:, :, :], axis=AX.X)
                v.drain()
                for e in range(NE):
                    v.tensor_tensor(gk1[:, :, e], ge[:, :, e], gm1[:, :], OP.is_ge)
                v.drain()
                v.tensor_tensor(gsc[:, :, :], ge[:, :, :], gk1[:, :, :], OP.mult)
                v.drain()
                v.tensor_tensor(gsc[:, :, :], ge[:, :, :], gsc[:, :, :], OP.subtract)
                v.drain()
                v.reduce_max(gm2[:, :], gsc[:, :, :], axis=AX.X)
                v.drain()
                for e in range(NE):
                    v.tensor_tensor(gk2[:, :, e], gsc[:, :, e], gm2[:, :], OP.is_ge)
                v.drain()
                v.tensor_tensor(gk1[:, :, :], gk1[:, :, :], gk2[:, :, :], OP.add)
                v.drain()
                v.tensor_tensor(gsc[:, :, :], ge[:, :, :], gk1[:, :, :], OP.mult)
                v.drain()
                v.reduce_sum(gm1[:, :], gsc[:, :, :], axis=AX.X)
                v.drain()
                v.reciprocal(gm1[:, :], gm1[:, :])
                v.drain()
                for e in range(NE):
                    v.tensor_tensor(gsc[:, :, e], gsc[:, :, e], gm1[:, :], OP.mult)
                v.drain()
                incl(v, "s_dve")
            sec("vector", _)
            m_w = cnt["s_dve"]

            for g16 in range(16):
                def _(t, g16=g16):
                    w8(t, "s_dve", m_w)
                    t.transpose(ps_big[0:NE, 128 * g16:128 * (g16 + 1)],
                                gsc[:, g16, :], eyef_t[:, :])
                    incl(t, "s_pe")
                sec("tensor", _)
            m_wtp = cnt["s_pe"]

            def _(s):
                w8(s, "s_pe", m_wtp)
                s.copy(lgw_sb[:, :], ps_big[0:NE, :])
                incl(s, "s_act")
            sec("scalar", _)
            m_wem = cnt["s_act"]
            flush()
        _p6b.close()

        _p6c = ExitStack()
        ps_bc = _p6c.enter_context(nc.psum_tensor("ps_bc", [128, 2, PIECE], F32))
        ps_eo = _p6c.enter_context(nc.psum_tensor("ps_eo", [128, 2, PIECE], F32))
        we_s = _p6c.enter_context(nc.sbuf_tensor("we_s", [128, 2, 2, C, PIECE], BF16))
        lgw_b = _p6c.enter_context(nc.sbuf_tensor("lgw_b", [NE, T], BF16))
        bc_sb = _p6c.enter_context(nc.sbuf_tensor("bc_sb", [128, 2, PIECE], BF16))
        if True:

            wsem = ["s_w0", "s_w1"]
            pe_bcm = {}
            act_bc = {}
            dve_xs = {}
            pe_eo = {}
            dve_eo = {}
            we0_mark = [0] * NE
            wh_mark = {}

            def stage_moe_w(e):
                eb = e % 2
                for mh in range(2):
                    hs = wsem[mh]

                    def _(g, e=e, eb=eb, mh=mh, hs=hs):
                        if e >= 2:
                            w8(g, "s_pe", pe_eo[(e - 2, NP - 1, 4 * mh + 3)])
                        else:
                            w8(g, "s_dve", xn2_done)
                        dma(g, hs, we_s[:, eb, mh, :, :],
                            wemt[e, :, PIECE * mh:PIECE * (mh + 1)]
                            .rearrange("(k p) m -> p k m", p=128))
                    sec("gpsimd", _)
                wh_mark[e] = [cnt["s_w0"], cnt["s_w1"]]

            def stage_gate_row(e):
                eb = e % 2

                def _(g, e=e, eb=eb):
                    w8(g, "s_act", m_wem)
                    if e >= 2:
                        w8(g, "s_pe", pe_eo[(e - 2, NP - 1, C - 1)])
                    we0_mark[e] = dma(g, "s_q", we0[0:1, eb, :],
                                      lgw_sb[e:e + 1, :])
                sec("gpsimd", _)

            stage_moe_w(0)
            stage_moe_w(1)

            def _(g):
                w8(g, "s_act", m_wem)
                dma(g, "s_q", lgw_b[:, :], lgw_sb[:, :])
                dma(g, "s_q", bem8[:, :],
                    bem.rearrange("o (e d) -> (o e) d", e=NE))
            sec("gpsimd", _)
            m_lgwb = cnt["s_q"]
            stage_gate_row(0)
            stage_gate_row(1)

            for e in range(NE):
                eb = e % 2

                for p in range(NP):
                    sl = slice(p * PIECE, (p + 1) * PIECE)
                    bs = e * NP + p
                    bb = bs % 2

                    def _(t, e=e, eb=eb, p=p, sl=sl, bs=bs, bb=bb):
                        w8(t, "s_q", we0_mark[e])
                        if bs >= 2:
                            w8(t, "s_act", act_bc[bs - 2])
                        t.matmul(ps_bc[:, bb, :], ones_rb[:, :],
                                 we0[0:1, eb, sl], start=True, stop=True)
                        pe_bcm[bs] = incl(t, "s_pe")
                    sec("tensor", _)

                    def _(s, e=e, p=p, sl=sl, bs=bs, bb=bb):
                        w8(s, "s_pe", pe_bcm[bs])
                        if bs >= 2:
                            w8(s, "s_dve", dve_xs[bs - 2])
                        s.activation(out=bc_sb[:, bb, :], in_=ps_bc[:, bb, :],
                                     func=AF.Copy)
                        act_bc[bs] = incl(s, "s_act")
                    sec("scalar", _)

                    def _(v, e=e, p=p, sl=sl, bs=bs, bb=bb):
                        w8(v, "s_act", act_bc[bs])
                        if e >= 1:
                            w8(v, "s_pe", pe_eo[(e - 1, p, C - 1)])
                        if e == 0:
                            w8(v, "s_dve", xn2_done)
                        for kc in range(C):
                            v.tensor_tensor(sbf[:, kc, sl], opnd[:, kc, sl],
                                            bc_sb[:, bb, :], OP.mult)
                        dve_xs[bs] = incl(v, "s_dve")
                    sec("vector", _)

                    for m in range(C):
                        mh = m // 4
                        es = (e * NP + p) * C + m
                        ob = es % 2

                        def _(t, e=e, eb=eb, p=p, sl=sl, m=m, mh=mh, es=es, ob=ob,
                              bs=bs):
                            w8(t, wsem[mh], wh_mark[e][mh])
                            w8(t, "s_dve", dve_xs[bs])
                            if es >= 2:
                                w8(t, "s_dve", dve_eo[es - 2])
                            if e == 0:
                                w8(t, "s_q", m_lgwb)
                                t.matmul(ps_eo[:, ob, :],
                                         bem8[0:NE, 128 * m:128 * (m + 1)],
                                         lgw_b[0:NE, sl], start=True, stop=False)
                            for kc in range(C):
                                t.matmul(ps_eo[:, ob, :],
                                         we_s[:, eb, mh, kc, 128 * (m - 4 * mh):128 * (m - 4 * mh + 1)],
                                         sbf[:, kc, sl], start=(e > 0 and kc == 0),
                                         stop=(kc == C - 1))
                            pe_eo[(e, p, m)] = incl(t, "s_pe")
                        sec("tensor", _)

                        def _(v, p=p, sl=sl, m=m, es=es, ob=ob):
                            w8(v, "s_pe", pe_eo[(e, p, m)])
                            v.scalar_tensor_tensor(xf[:, m, sl], ps_eo[:, ob, :], 1.0,
                                                   xf[:, m, sl], OP.mult, OP.add)
                            dve_eo[es] = incl(v, "s_dve")
                        sec("vector", _)

                if e + 2 < NE:
                    stage_moe_w(e + 2)
                    stage_gate_row(e + 2)
                flush()
            moe_done = dve_eo[NE * NP * C - 1]
        _p6c.close()
    _p6.close()

    def _(sy):
        w8(sy, "s_dve", moe_done)
        m = 0
        for c in range(C):
            m = dma(sy, "s_st", out_f[128 * c:128 * (c + 1), :], xf[:, c, :])
        w8(sy, "s_st", m)
    sec("sync", _)

    flush()
    ctx.close()
    return nc


_NC_CACHE = {}


def _get_nc():
    key = (DEBUG_TAPS, BUILD_PHASES)
    if key not in _NC_CACHE:
        _NC_CACHE[key] = build_nc()
    return _NC_CACHE[key]


def _tr(a):
    return np.ascontiguousarray(np.asarray(a).T)


def _fm(vec):
    return np.ascontiguousarray(np.asarray(vec).reshape(C, 128).T)


def kernel(L0_attn_x, L0_attn_w_rms_mix, L0_attn_Wg, L0_attn_bg, L0_attn_Wv,
           L0_attn_bv, L0_attn_Wd, L0_attn_bd, L0_attn_w_rms_attn, L0_attn_Wqkv,
           L0_attn_Wo, L0_attn_w_rms_ffn, L0_attn_Wgate, L0_attn_Wup,
           L0_attn_Wout, L0_ffn_Wg, L0_ffn_bg, L0_ffn_We, L0_ffn_be):
    x = np.asarray(L0_attn_x).reshape(16384, D)

    jj = np.arange(128)[:, None]
    kidx = np.arange(256)[None, :]
    allowed = (kidx <= 128 + jj) & (kidx >= jj + 1)
    mask_std = np.tile(np.where(allowed, 0.0, NEG).astype(ml_dtypes.bfloat16), (1, 2))
    mask_first0 = np.tile(np.where(allowed & (kidx >= 128), 0.0, NEG)
                          .astype(ml_dtypes.bfloat16), (1, 2))

    shared = {
        "wgt": _tr(L0_attn_Wg), "wvt": _tr(L0_attn_Wv), "wdt": _tr(L0_attn_Wd),
        "wqkvt": _tr(L0_attn_Wqkv), "wot": _tr(L0_attn_Wo),
        "wgatet": _tr(L0_attn_Wgate), "wupt": _tr(L0_attn_Wup),
        "woutt": _tr(L0_attn_Wout),
        "wemt": np.ascontiguousarray(np.asarray(L0_ffn_We).transpose(0, 2, 1)),
        "wgm": np.ascontiguousarray(np.asarray(L0_ffn_Wg)),
        "wmix": _fm(L0_attn_w_rms_mix), "wattn": _fm(L0_attn_w_rms_attn),
        "wffn": _fm(L0_attn_w_rms_ffn),
        "bg3": _fm(L0_attn_bg), "bv3": _fm(L0_attn_bv), "bd3": _fm(L0_attn_bd),
        "bgm": np.ascontiguousarray(np.asarray(L0_ffn_bg).reshape(NE, 1)),
        "bem": np.ascontiguousarray(np.asarray(L0_ffn_be).reshape(1, NE * D)),
        "eye_bf": np.eye(128, dtype=ml_dtypes.bfloat16),
        "eye_f": np.eye(128, dtype=np.float32),
        "mask_std": mask_std,
    }

    in_maps = []
    for r in range(R):
        im = dict(shared)
        im["x_f"] = np.ascontiguousarray(x[T * r:T * (r + 1), :].T)
        im["mask_first"] = mask_first0 if r == 0 else mask_std
        selc = np.zeros((128, R), dtype=np.float32); selc[:, r] = 1.0
        selh = np.zeros((128, R), dtype=np.float32)
        if r > 0:
            selh[:, r - 1] = 1.0
        im["sel_carry"] = selc
        im["sel_halo"] = selh
        in_maps.append(im)

    nc = _get_nc()
    res = run_bass_kernel_spmd(nc, in_maps, core_ids=list(range(R)))
    out = np.concatenate([np.asarray(res.results[r]["out_f"]).T for r in range(R)],
                         axis=0)
    if DEBUG_TAPS:
        kernel.last_results = res.results
    return np.ascontiguousarray(out.reshape(1, 16384, D)).astype(np.float32)


# version 14 (final)
# speedup vs baseline: 2.1271x; 1.0148x over previous
import numpy as np
import ml_dtypes
from contextlib import ExitStack

import concourse.bass as bass
import concourse.mybir as mybir
from concourse.bass_utils import run_bass_kernel_spmd
from concourse.alu_op_type import AluOpType as OP

F32 = mybir.dt.float32
BF16 = mybir.dt.bfloat16
AF = mybir.ActivationFunctionType
AX = mybir.AxisListType

R = 8
T = 2048
C = 8
D = 1024
PIECE = 512
NP = T // PIECE
PAIRS = 8
NCH = T // 128
DFF = 4096
NJ = 8
NE = 8
NEG = -1.0e30
LN_EPS = 1e-5
RMS_EPS = 1e-6

G4LIM = 9
DEBUG_TAPS = ()
BUILD_PHASES = 7


def build_nc():
    nc = bass.Bass("TRN2", target_bir_lowering=False, detect_race_conditions=False)

    def IN(name, shape, dt=F32):
        return nc.dram_tensor(name, list(shape), dt, kind="ExternalInput").ap()

    x_f = IN("x_f", [D, T])
    wgt = IN("wgt", [D, D]); wvt = IN("wvt", [D, D]); wdt = IN("wdt", [D, D])
    wqkvt = IN("wqkvt", [D, 3 * D])
    wot = IN("wot", [D, D])
    wgatet = IN("wgatet", [D, DFF]); wupt = IN("wupt", [D, DFF])
    woutt = IN("woutt", [DFF, D])
    wemt = IN("wemt", [NE, D, D])
    wgm = IN("wgm", [D, NE])
    wmix = IN("wmix", [128, C]); wattn = IN("wattn", [128, C]); wffn = IN("wffn", [128, C])
    bg3 = IN("bg3", [128, C]); bv3 = IN("bv3", [128, C]); bd3 = IN("bd3", [128, C])
    bgm = IN("bgm", [NE, 1])
    bem = IN("bem", [1, NE * D])
    eye_bf = IN("eye_bf", [128, 128], BF16)
    eye_f = IN("eye_f", [128, 128])
    mask_std = IN("mask_std", [128, 512], BF16)
    mask_first = IN("mask_first", [128, 512], BF16)
    sel_carry = IN("sel_carry", [128, R])
    sel_halo = IN("sel_halo", [128, R])

    out_f = nc.dram_tensor("out_f", [D, T], F32, kind="ExternalOutput").ap()
    taps = {t: nc.dram_tensor("tap_" + t, [D, T], F32, kind="ExternalOutput").ap()
            for t in DEBUG_TAPS}

    cc_sc_in = [nc.dram_tensor(f"cc_sc_in{c}", [128, 2], F32).ap() for c in range(C)]
    cc_sc_out = [nc.dram_tensor(f"cc_sc_out{c}", [R * 128, 2], F32).ap() for c in range(C)]
    cc_kv_in = nc.dram_tensor("cc_kv_in", [128, 2 * PAIRS * 128], BF16).ap()
    cc_kv_out = nc.dram_tensor("cc_kv_out", [R * 128, 2 * PAIRS * 128], BF16).ap()

    ctx = ExitStack()
    E = ctx.enter_context

    xf = E(nc.sbuf_tensor("xf", [128, C, T], F32))
    opnd = E(nc.sbuf_tensor("opnd", [128, C, T], BF16))
    sbf = E(nc.sbuf_tensor("sbf", [128, C, T], BF16))
    wmix_t = E(nc.sbuf_tensor("wmix_t", [128, C], F32))
    wattn_t = E(nc.sbuf_tensor("wattn_t", [128, C], F32))
    wffn_t = E(nc.sbuf_tensor("wffn_t", [128, C], F32))
    bg_t = E(nc.sbuf_tensor("bg_t", [128, C], F32))
    bv_t = E(nc.sbuf_tensor("bv_t", [128, C], F32))
    bd_t = E(nc.sbuf_tensor("bd_t", [128, C], F32))
    bgm_t = E(nc.sbuf_tensor("bgm_t", [NE, 1], F32))
    eyeb_t = E(nc.sbuf_tensor("eyeb_t", [128, 128], BF16))
    eyef_t = E(nc.sbuf_tensor("eyef_t", [128, 128], F32))
    mstd_t = E(nc.sbuf_tensor("mstd_t", [128, 512], BF16))
    mfirst_t = E(nc.sbuf_tensor("mfirst_t", [128, 512], BF16))
    selc_t = E(nc.sbuf_tensor("selc_t", [128, R], F32))
    selh_t = E(nc.sbuf_tensor("selh_t", [128, R], F32))
    ones_r = E(nc.sbuf_tensor("ones_r", [1, 128], F32))
    ones_b = E(nc.sbuf_tensor("ones_b", [128, 1], BF16))
    eps1_t = E(nc.sbuf_tensor("eps1_t", [1, 1], F32))
    eps2_t = E(nc.sbuf_tensor("eps2_t", [1, 1], F32))
    ones_rb = E(nc.sbuf_tensor("ones_rb", [1, 128], BF16))
    _pw = ExitStack()
    wg_s = _pw.enter_context(nc.sbuf_tensor("wg_s", [128, 2, C, 128], BF16))
    wv_s = _pw.enter_context(nc.sbuf_tensor("wv_s", [128, 2, C, 128], BF16))
    wd_s = _pw.enter_context(nc.sbuf_tensor("wd_s", [128, 2, C, 128], BF16))

    block = E(nc.Block())

    SEMS = ("s_x", "s_w0", "s_w1", "s_pe", "s_act", "s_dve",
            "s_cc", "s_g0", "s_g1", "s_q", "s_st", "s_pool", "s_pub")
    S = {n: E(nc.semaphore(n)) for n in SEMS}
    cnt = {n: 0 for n in SEMS}

    def inc(inst, sn, n=1):
        inst.then_inc(S[sn], n)
        cnt[sn] += n
        return cnt[sn]

    def dma(eng, sn, out, in_):
        return inc(eng.dma_start(out=out, in_=in_), sn, 16)

    def w8(eng, sn, v):
        if not _pass2[0]:
            assert v <= cnt[sn], (sn, v, cnt[sn])
        if v > 0:
            eng.wait_ge(S[sn], v)

    _NOREC = {"wait_ge", "nop", "dma_start", "sem_inc", "sem_clear", "drain",
              "dma_start_transpose"}

    class _EngRec:
        def __init__(self, eng):
            self._e = eng
            self.last = None
        def __getattr__(self, n):
            f = getattr(self._e, n)
            if callable(f) and not n.startswith("_"):
                rec = self
                def w(*a, _f=f, _n=n, **k):
                    r = _f(*a, **k)
                    if _n not in _NOREC and hasattr(r, "then_inc"):
                        rec.last = r
                    return r
                return w
            return f

    class _MockInst:
        def then_inc(self, *a, **k):
            return self

    _MOCK_INST = _MockInst()

    class _MockEng:
        last = _MOCK_INST
        def __getattr__(self, n):
            return lambda *a, **k: _MOCK_INST

    _MOCK = _MockEng()

    _stage_buf = []
    _span_cnt = [dict(cnt)]
    _pass2 = [False]

    def stage(engn, fn):
        _stage_buf.append((engn, fn))
        fn(_MOCK)

    sec = stage

    def flush():
        if not _stage_buf:
            return
        end_cnt = dict(cnt)
        cnt.update(_span_cnt[0])
        _pass2[0] = True
        for engn in ("sync", "gpsimd", "tensor", "scalar", "vector"):
            fns = [f for e, f in _stage_buf if e == engn]
            if not fns:
                continue
            def body(eng, fns=fns):
                w = _EngRec(eng)
                for f in fns:
                    f(w)
            getattr(block, engn)(body)
        _pass2[0] = False
        assert cnt == end_cnt, (cnt, end_cnt)
        _stage_buf.clear()
        _span_cnt[0] = dict(cnt)

    def incl(engw, sn, n=1):
        assert engw.last is not None, "incl with no prior real instruction"
        return inc(engw.last, sn, n)

    def _(sy):
        dma(sy, "s_x", wmix_t[:, :], wmix)
        dma(sy, "s_x", wattn_t[:, :], wattn)
        dma(sy, "s_x", wffn_t[:, :], wffn)
        dma(sy, "s_x", bg_t[:, :], bg3)
        dma(sy, "s_x", bv_t[:, :], bv3)
        dma(sy, "s_x", bd_t[:, :], bd3)
        dma(sy, "s_x", bgm_t[:, :], bgm)
        dma(sy, "s_x", eyeb_t[:, :], eye_bf)
        dma(sy, "s_x", eyef_t[:, :], eye_f)
        dma(sy, "s_x", mstd_t[:, :], mask_std)
        dma(sy, "s_x", mfirst_t[:, :], mask_first)
        dma(sy, "s_x", selc_t[:, :], sel_carry)
        dma(sy, "s_x", selh_t[:, :], sel_halo)
        for c in range(C):
            dma(sy, "s_x", xf[:, c, :], x_f[128 * c:128 * (c + 1), :])
    sec("sync", _)
    mark_const = cnt["s_x"]

    def _(v):
        v.memset(ones_r[:, :], 1.0)
        v.memset(ones_b[:, :], 1.0)
        v.memset(eps1_t[:, :], LN_EPS)
        v.memset(eps2_t[:, :], RMS_EPS)
        v.memset(ones_rb[:, :], 1.0)
        incl(v, "s_dve")
    sec("vector", _)
    mark_ones = cnt["s_dve"]
    flush()

    def tap_xf(name, prev_dve_mark):
        if name not in taps:
            return 0
        def _(g):
            w8(g, "s_dve", prev_dve_mark)
            for c in range(C):
                dma(g, "s_st", taps[name][128 * c:128 * (c + 1), :], xf[:, c, :])
        sec("gpsimd", _)
        return cnt["s_st"]

    def tap_opnd(name, prev_dve_mark):
        if name not in taps:
            return 0
        def _(g):
            w8(g, "s_dve", prev_dve_mark)
            for c in range(C):
                dma(g, "s_st", taps[name][128 * c:128 * (c + 1), :], opnd[:, c, :])
        sec("gpsimd", _)
        return cnt["s_st"]


    def _finish(done_mark):
        flush()
        def _(sy):
            w8(sy, "s_dve", done_mark)
            m = 0
            for c in range(C):
                m = dma(sy, "s_st", out_f[128 * c:128 * (c + 1), :], xf[:, c, :])
            w8(sy, "s_st", m)
        sec("sync", _)
        flush()
        ctx.close()
        return nc

    if BUILD_PHASES <= 0:
        return _finish(mark_ones)

    ph1 = {}
    _p1 = ExitStack()
    ps_mu = _p1.enter_context(nc.psum_tensor("ps_mu", [1, 2, PIECE], F32))
    ps_ms = _p1.enter_context(nc.psum_tensor("ps_ms", [1, 2, PIECE], F32))
    ps_b1 = _p1.enter_context(nc.psum_tensor("ps_b1", [128, PIECE], F32))
    ps_b2 = _p1.enter_context(nc.psum_tensor("ps_b2", [128, PIECE], F32))
    ps_b3 = _p1.enter_context(nc.psum_tensor("ps_b3", [128, PIECE], F32))
    cast_s = _p1.enter_context(nc.sbuf_tensor("cast_s", [128, 2, PIECE], BF16))
    sq_s = _p1.enter_context(nc.sbuf_tensor("sq_s", [128, 2, PIECE], BF16))
    tmp_s = _p1.enter_context(nc.sbuf_tensor("tmp_s", [128, PIECE], F32))
    rows = _p1.enter_context(nc.sbuf_tensor("rows1", [1, 5, PIECE], F32))
    if True:

        pe_stats = [0] * NP
        dve_var = [0] * NP
        pe_bc = [0] * NP
        dve_apply = [0] * NP
        pe_cs = {}
        act_cs = {}

        for p in range(NP):
            sl = slice(p * PIECE, (p + 1) * PIECE)
            pp = p % 2

            for c in range(C):
                cs = p * C + c

                def _(s, p=p, c=c, cs=cs, sl=sl):
                    if cs == 0:
                        w8(s, "s_x", mark_const)
                        w8(s, "s_dve", mark_ones)
                    if cs >= 2:
                        w8(s, "s_pe", pe_cs[cs - 2])
                    s.activation(out=cast_s[:, cs % 2, :], in_=xf[:, c, sl], func=AF.Copy)
                    s.activation(out=sq_s[:, cs % 2, :], in_=xf[:, c, sl], func=AF.Square)
                    act_cs[cs] = incl(s, "s_act")
                sec("scalar", _)

                def _(t, p=p, c=c, cs=cs, pp=pp):
                    w8(t, "s_act", act_cs[cs])
                    if c == 0:
                        w8(t, "s_dve", mark_ones)
                        if p >= 2:
                            w8(t, "s_dve", dve_var[p - 2])
                    t.matmul(ps_mu[0:1, pp, :], ones_b[:, :], cast_s[:, cs % 2, :],
                             start=(c == 0), stop=(c == C - 1))
                    t.matmul(ps_ms[0:1, pp, :], ones_b[:, :], sq_s[:, cs % 2, :],
                             start=(c == 0), stop=(c == C - 1))
                    pe_cs[cs] = incl(t, "s_pe")
                pe_stats[p] = None
                sec("tensor", _)
            pe_stats[p] = cnt["s_pe"]

            def _(v, p=p, pp=pp):
                w8(v, "s_pe", pe_stats[p])
                if p >= 1:
                    w8(v, "s_pe", pe_bc[p - 1])
                v.tensor_scalar(rows[0:1, 0, :], ps_mu[0:1, pp, :], 1.0 / D, None, OP.mult)
                v.tensor_scalar(rows[0:1, 4, :], ps_ms[0:1, pp, :], 1.0 / D, None, OP.mult)
                v.tensor_tensor(rows[0:1, 1, :], rows[0:1, 0, :], rows[0:1, 0, :], OP.mult)
                v.tensor_tensor(rows[0:1, 1, :], rows[0:1, 4, :], rows[0:1, 1, :], OP.subtract)
                dve_var[p] = incl(v, "s_dve")
            sec("vector", _)

            def _(s, p=p):
                w8(s, "s_dve", dve_var[p])
                s.activation(out=rows[0:1, 4, :], in_=rows[0:1, 1, :], func=AF.Ln, bias=eps1_t[0:1, :])
                s.drain()
                s.activation(out=rows[0:1, 2, :], in_=rows[0:1, 4, :], func=AF.Exp, scale=-0.5)
                incl(s, "s_act")
            sec("scalar", _)
            m_r1 = cnt["s_act"]

            def _(v, m_r1=m_r1):
                w8(v, "s_act", m_r1)
                v.tensor_tensor(rows[0:1, 0, :], rows[0:1, 0, :], rows[0:1, 2, :], OP.mult)
                v.tensor_tensor(rows[0:1, 4, :], rows[0:1, 2, :], rows[0:1, 2, :], OP.mult)
                v.tensor_tensor(rows[0:1, 3, :], rows[0:1, 1, :], rows[0:1, 4, :], OP.mult)
                incl(v, "s_dve")
            sec("vector", _)
            m_ms2 = cnt["s_dve"]

            def _(s, m_ms2=m_ms2):
                w8(s, "s_dve", m_ms2)
                s.activation(out=rows[0:1, 3, :], in_=rows[0:1, 3, :], func=AF.Ln, bias=eps2_t[0:1, :])
                s.drain()
                s.activation(out=rows[0:1, 3, :], in_=rows[0:1, 3, :], func=AF.Exp, scale=-0.5)
                incl(s, "s_act")
            sec("scalar", _)
            m_r2 = cnt["s_act"]

            def _(t, p=p, m_r2=m_r2):
                w8(t, "s_act", m_r2)
                if p >= 1:
                    w8(t, "s_dve", dve_apply[p - 1])
                t.matmul(ps_b1[:, :], ones_r[:, :], rows[0:1, 2, :], start=True, stop=True)
                t.matmul(ps_b2[:, :], ones_r[:, :], rows[0:1, 0, :], start=True, stop=True)
                t.matmul(ps_b3[:, :], ones_r[:, :], rows[0:1, 3, :], start=True, stop=True)
                pe_bc[p] = incl(t, "s_pe")
            sec("tensor", _)

            def _(v, p=p, sl=sl):
                w8(v, "s_pe", pe_bc[p])
                for c in range(C):
                    v.tensor_tensor(tmp_s[:, :], xf[:, c, sl], ps_b1[:, :], OP.mult)
                    v.tensor_tensor(xf[:, c, sl], tmp_s[:, :], ps_b2[:, :], OP.subtract)
                    v.tensor_tensor(tmp_s[:, :], xf[:, c, sl], ps_b3[:, :], OP.mult)
                    v.tensor_scalar(opnd[:, c, sl], tmp_s[:, :], wmix_t[:, c:c + 1], None, OP.mult)
                dve_apply[p] = incl(v, "s_dve")
            sec("vector", _)

        ph1["done"] = dve_apply[NP - 1]
        ph1["applies"] = list(dve_apply)
    flush()
    _p1.close()

    ln_done = ph1["done"]
    tg = max(tap_xf("xn", ln_done), tap_opnd("xm", ln_done))
    if BUILD_PHASES <= 1:
        return _finish(ln_done)

    ph2 = {}
    _p2 = ExitStack()
    ps_g = _p2.enter_context(nc.psum_tensor("ps_g", [128, 2, PIECE], F32))
    ps_v = _p2.enter_context(nc.psum_tensor("ps_v", [128, 2, PIECE], F32))
    ps_d = _p2.enter_context(nc.psum_tensor("ps_d", [128, 2, PIECE], F32))
    a_s = _p2.enter_context(nc.sbuf_tensor("a_s", [128, 2, T], F32))
    xs_s = _p2.enter_context(nc.sbuf_tensor("xs_s", [128, 2, T], F32))
    sg_s = _p2.enter_context(nc.sbuf_tensor("sg_s", [128, PIECE], F32))
    tv_s = _p2.enter_context(nc.sbuf_tensor("tv_s", [128, PIECE], F32))
    dum_s = _p2.enter_context(nc.sbuf_tensor("dum_s", [128, PIECE], F32))
    h_s = _p2.enter_context(nc.sbuf_tensor("h_s", [128, T], F32))
    pub_s = _p2.enter_context(nc.sbuf_tensor("pub_s", [128, 2, 2], F32))
    gath_s = _p2.enter_context(nc.sbuf_tensor("gath_s", [128, 2, 2 * R], F32))
    st_s = _p2.enter_context(nc.sbuf_tensor("st_s", [128, 4], F32))
    ls_s = _p2.enter_context(nc.sbuf_tensor("ls_s", [128, 8], F32))
    if True:

        wsem = ["s_w0", "s_w1"]
        gsem = ["s_g0", "s_g1"]
        pe_gvd = [[0] * NP for _ in range(C)]
        act_gvd = [[0] * NP for _ in range(C)]
        dve_axs = [[0] * NP for _ in range(C)]
        dve_scan1 = [0] * C
        act_atot = [0] * C
        dve_y1 = [0] * C
        gath_mark = [0] * C
        cc_mark = [0] * C
        w_mark = [0] * C

        def emit_tail(c):
            cb = c % 2

            def _(v, c=c, cb=cb):
                w8(v, gsem[cb], gath_mark[c])
                v.memset(st_s[:, 0:1], 0.0)
                v.memset(st_s[:, 2:3], 0.0)
                cur = 0
                for j in range(R):
                    v.scalar_tensor_tensor(st_s[:, 2:3], st_s[:, cur:cur + 1],
                                           selc_t[:, j:j + 1], st_s[:, 2:3],
                                           OP.mult, OP.add)
                    if j < R - 1:
                        nxt = 1 - cur
                        v.scalar_tensor_tensor(st_s[:, nxt:nxt + 1], st_s[:, cur:cur + 1],
                                               gath_s[:, cb, 2 * j:2 * j + 1],
                                               gath_s[:, cb, 2 * j + 1:2 * j + 2],
                                               OP.mult, OP.add)
                        cur = nxt
                v.drain()
                v.tensor_tensor_scan(h_s[:, :], a_s[:, cb, :], xs_s[:, cb, :],
                                     st_s[:, 2:3], OP.mult, OP.add)
                v.drain()
                v.tensor_tensor(xf[:, c, :], xf[:, c, :], h_s[:, :], OP.add)
                dve_y1[c] = incl(v, "s_dve")
            sec("vector", _)

        def stage_gru_w(c):
            cb = c % 2
            ws = wsem[cb]

            def _(g, c=c, cb=cb):
                if c >= 2:
                    w8(g, "s_pe", pe_gvd[c - 2][NP - 1])
                dma(g, ws, wg_s[:, cb, :, :],
                    wgt[:, 128 * c:128 * (c + 1)].rearrange("(k p) m -> p k m", p=128))
                dma(g, ws, wv_s[:, cb, :, :],
                    wvt[:, 128 * c:128 * (c + 1)].rearrange("(k p) m -> p k m", p=128))
                dma(g, ws, wd_s[:, cb, :, :],
                    wdt[:, 128 * c:128 * (c + 1)].rearrange("(k p) m -> p k m", p=128))
            sec("gpsimd", _)
            w_mark[c] = cnt[ws]

        stage_gru_w(0)
        stage_gru_w(1)
        for c in range(C):
            cb = c % 2
            ws = wsem[cb]
            gs = gsem[cb]

            for p in range(NP):
                sl = slice(p * PIECE, (p + 1) * PIECE)
                pp = p % 2
                step = c * NP + p

                def _(t, c=c, cb=cb, p=p, pp=pp, sl=sl, ws=ws, step=step):
                    w8(t, ws, w_mark[c])
                    if c == 0 and p == 0:
                        w8(t, "s_dve", ln_done)
                        if tg:
                            w8(t, "s_st", tg)
                    prev = step - 2
                    if prev >= 0:
                        w8(t, "s_act", act_gvd[prev // NP][prev % NP])
                    for kc in range(C):
                        st0 = (kc == 0); sp = (kc == C - 1)
                        t.matmul(ps_g[:, pp, :], wg_s[:, cb, kc, :], opnd[:, kc, sl], start=st0, stop=sp)
                        t.matmul(ps_v[:, pp, :], wv_s[:, cb, kc, :], opnd[:, kc, sl], start=st0, stop=sp)
                        t.matmul(ps_d[:, pp, :], wd_s[:, cb, kc, :], opnd[:, kc, sl], start=st0, stop=sp)
                    pe_gvd[c][p] = incl(t, "s_pe")
                sec("tensor", _)

                def _(s, c=c, cb=cb, p=p, pp=pp, sl=sl, step=step):
                    w8(s, "s_pe", pe_gvd[c][p])
                    if step >= 1:
                        prev = step - 1
                        w8(s, "s_dve", dve_axs[prev // NP][prev % NP])
                    if p == 0 and c >= 2:
                        w8(s, "s_dve", dve_y1[c - 2])
                    s.activation(out=sg_s[:, :], in_=ps_g[:, pp, :], func=AF.Sigmoid,
                                 bias=bg_t[:, c:c + 1], scale=1.0)
                    s.activation(out=tv_s[:, :], in_=ps_v[:, pp, :], func=AF.Tanh,
                                 bias=bv_t[:, c:c + 1], scale=1.0)
                    s.activation(out=a_s[:, cb, sl], in_=ps_d[:, pp, :], func=AF.Sigmoid,
                                 bias=bd_t[:, c:c + 1], scale=1.0)
                    act_gvd[c][p] = incl(s, "s_act")
                sec("scalar", _)

                def _(v, c=c, cb=cb, p=p, sl=sl):
                    w8(v, "s_act", act_gvd[c][p])
                    v.tensor_tensor(xs_s[:, cb, sl], sg_s[:, :], tv_s[:, :], OP.mult)
                    v.tensor_scalar(a_s[:, cb, sl], a_s[:, cb, sl], 0.998, 0.001,
                                    OP.mult, OP.add)
                    dve_axs[c][p] = incl(v, "s_dve")
                sec("vector", _)

            if c + 2 < C:
                stage_gru_w(c + 2)

            def _(v, c=c, cb=cb, gs=gs):
                w8(v, "s_dve", dve_axs[c][NP - 1])
                if c >= 2:
                    w8(v, gs, gath_mark[c - 2])
                v.drain()
                v.tensor_tensor_scan(h_s[:, :], a_s[:, cb, :], xs_s[:, cb, :],
                                     0.0, OP.mult, OP.add)
                v.drain()
                v.tensor_copy(pub_s[:, cb, 1:2], h_s[:, T - 1:T])
                dve_scan1[c] = incl(v, "s_dve")
            sec("vector", _)

            def _(s, c=c, cb=cb, gs=gs):
                w8(s, "s_dve", dve_axs[c][NP - 1])
                if c >= 2:
                    w8(s, gs, gath_mark[c - 2])
                for i in range(NP):
                    s.activation(out=dum_s[:, :], in_=a_s[:, cb, i * PIECE:(i + 1) * PIECE],
                                 func=AF.Ln, accum_out=ls_s[:, i:i + 1])
                s.drain()
                s.activation(out=ls_s[:, 4:5], in_=ls_s[:, 0:1], func=AF.Identity,
                             bias=ls_s[:, 1:2], scale=1.0)
                s.activation(out=ls_s[:, 5:6], in_=ls_s[:, 2:3], func=AF.Identity,
                             bias=ls_s[:, 3:4], scale=1.0)
                s.drain()
                s.activation(out=pub_s[:, cb, 0:1], in_=ls_s[:, 4:5], func=AF.Exp,
                             bias=ls_s[:, 5:6], scale=1.0)
                act_atot[c] = incl(s, "s_act")
            sec("scalar", _)

            def _(g, c=c, cb=cb, gs=gs):
                w8(g, "s_dve", dve_scan1[c])
                w8(g, "s_act", act_atot[c])
                m = dma(g, "s_pub", cc_sc_in[c][:, :], pub_s[:, cb, :])
                w8(g, "s_pub", m)
                if c >= 1:
                    w8(g, "s_cc", cc_mark[c - 1])
                cc = g.collective_compute(
                    "AllGather", OP.bypass,
                    replica_groups=[list(range(R))],
                    ins=[cc_sc_in[c][:, :]],
                    outs=[cc_sc_out[c][:, :]],
                )
                cc_mark[c] = inc(cc, "s_cc")
            sec("gpsimd", _)

            def _(sy, c=c, cb=cb, gs=gs):
                w8(sy, "s_cc", cc_mark[c])
                if c >= 2:
                    w8(sy, "s_dve", dve_y1[c - 2])
                gath_mark[c] = dma(
                    sy, gs, gath_s[:, cb, :].rearrange("p (j s) -> p j s", j=R),
                    cc_sc_out[c].rearrange("(j p) s -> p j s", p=128))
            sec("sync", _)

            if c >= 1:
                emit_tail(c - 1)
                flush()
        emit_tail(C - 1)
        flush()
        ph2["done"] = dve_y1[C - 1]
    _p2.close()
    _pw.close()

    gru_done = ph2["done"]
    tg = tap_xf("y1", gru_done)
    if BUILD_PHASES <= 2:
        return _finish(gru_done)

    def rms_pass(w_t, prev_dve, tag, st_gate=0):
        res = {}
        _pr = ExitStack()
        ps_st = _pr.enter_context(nc.psum_tensor(f"ps_st{tag}", [1, NP, PIECE], F32))
        ps_bb = _pr.enter_context(nc.psum_tensor(f"ps_bb{tag}", [128, 2, PIECE], F32))
        sq2 = _pr.enter_context(nc.sbuf_tensor(f"sq{tag}", [128, 2, PIECE], BF16))
        tp2 = _pr.enter_context(nc.sbuf_tensor(f"tp{tag}", [128, PIECE], F32))
        rows = _pr.enter_context(nc.sbuf_tensor(f"rows{tag}", [1, 5, PIECE], F32))
        if True:
            pe_st = [0] * NP
            pe_bb = [0] * NP
            act_r = [0] * NP
            dve_ap = [0] * NP
            pe_cs2 = {}
            act_cs2 = {}
            for p in range(NP):
                sl = slice(p * PIECE, (p + 1) * PIECE)
                pp = p % 2

                for c in range(C):
                    cs = p * C + c

                    def _(s, p=p, c=c, cs=cs, sl=sl):
                        if cs == 0:
                            w8(s, "s_dve", prev_dve)
                            if st_gate:
                                w8(s, "s_st", st_gate)
                        if cs >= 2:
                            w8(s, "s_pe", pe_cs2[cs - 2])
                        s.activation(out=sq2[:, cs % 2, :], in_=xf[:, c, sl],
                                     func=AF.Square)
                        act_cs2[cs] = incl(s, "s_act")
                    sec("scalar", _)

                    def _(t, p=p, c=c, cs=cs):
                        w8(t, "s_act", act_cs2[cs])
                        if cs == 0:
                            w8(t, "s_dve", mark_ones)
                        t.matmul(ps_st[0:1, p, :], ones_b[:, :], sq2[:, cs % 2, :],
                                 start=(c == 0), stop=(c == C - 1))
                        pe_cs2[cs] = incl(t, "s_pe")
                    sec("tensor", _)
                pe_st[p] = cnt["s_pe"]

                def _(s, p=p):
                    w8(s, "s_pe", pe_st[p])
                    if p >= 1:
                        w8(s, "s_pe", pe_bb[p - 1])
                    s.activation(out=rows[0:1, 4, :], in_=ps_st[0:1, p, :], func=AF.Ln,
                                 bias=eps2_t[0:1, :], scale=1.0 / D)
                    s.drain()
                    s.activation(out=rows[0:1, 3, :], in_=rows[0:1, 4, :], func=AF.Exp,
                                 scale=-0.5)
                    act_r[p] = incl(s, "s_act")
                sec("scalar", _)

                def _(t, p=p, pp=pp):
                    w8(t, "s_act", act_r[p])
                    if p >= 2:
                        w8(t, "s_dve", dve_ap[p - 2])
                    t.matmul(ps_bb[:, pp, :], ones_r[:, :], rows[0:1, 3, :],
                             start=True, stop=True)
                    pe_bb[p] = incl(t, "s_pe")
                sec("tensor", _)

                def _(v, p=p, pp=pp, sl=sl):
                    w8(v, "s_pe", pe_bb[p])
                    for c in range(C):
                        v.tensor_tensor(tp2[:, :], xf[:, c, sl], ps_bb[:, pp, :], OP.mult)
                        v.tensor_scalar(opnd[:, c, sl], tp2[:, :], w_t[:, c:c + 1],
                                        None, OP.mult)
                    dve_ap[p] = incl(v, "s_dve")
                sec("vector", _)
            res["done"] = dve_ap[NP - 1]
        flush()
        _pr.close()
        return res["done"]

    ya_done = rms_pass(wattn_t, gru_done, "a", st_gate=tg)
    tg = tap_opnd("ya", ya_done)
    if BUILD_PHASES <= 3:
        return _finish(ya_done)

    ph4 = {}
    _p4 = ExitStack()
    ps_qkv = _p4.enter_context(nc.psum_tensor("ps_qkv", [128, 2, PIECE], F32))
    ps_sc = _p4.enter_context(nc.psum_tensor("ps_sc", [128, 4, PIECE], F32))
    ps_pt = _p4.enter_context(nc.psum_tensor("ps_pt", [128, 2, 1024], BF16))
    wq_s = _p4.enter_context(nc.sbuf_tensor("wq_s", [128, 2, C, 128], BF16))
    wk_s = _p4.enter_context(nc.sbuf_tensor("wk_s", [128, 2, C, 128], BF16))
    wv2_s = _p4.enter_context(nc.sbuf_tensor("wv2_s", [128, 2, C, 128], BF16))
    qp_sb = _p4.enter_context(nc.sbuf_tensor("qp_sb", [128, 2, 2, T], BF16))
    k_sb = _p4.enter_context(nc.sbuf_tensor("k_sb", [128, 2, T], BF16))
    v_st = _p4.enter_context(nc.sbuf_tensor("v_st", [128, 2, PIECE], BF16))
    vt_sb = _p4.enter_context(nc.sbuf_tensor("vt_sb", [128, 2, NCH, 128], BF16))
    prb_sb = _p4.enter_context(nc.sbuf_tensor("prb_sb", [128, 4, 2, 256], BF16))
    prt_sb = _p4.enter_context(nc.sbuf_tensor("prt_sb", [128, 2, 4, 128], BF16))
    smx_sb = _p4.enter_context(nc.sbuf_tensor("smx_sb", [128, 4, 6], F32))
    gkv_sb = _p4.enter_context(nc.sbuf_tensor("gkv_sb", [128, 2, 1024], BF16))
    ret_qp = _p4.enter_context(nc.sbuf_tensor("ret_qp", [128, PAIRS, 2, 128], BF16))
    ret_k = _p4.enter_context(nc.sbuf_tensor("ret_k", [128, PAIRS, 256], BF16))
    ret_vt = _p4.enter_context(nc.sbuf_tensor("ret_vt", [128, PAIRS, 2, 128], BF16))
    if True:
        wsem = ["s_w0", "s_w1"]
        from collections import defaultdict as _dd
        G_pe_sc = _dd(int); G_dve_mr = _dd(int); G_act_exp = _dd(int); G_dve_rc = _dd(int)
        G_pool = _dd(int); G_pe_pt = _dd(int); G_dve_ptc = _dd(int); G_pe_av = _dd(int); G_dve_avc = _dd(int)
        gcnt = [0]
        act_qkv = {}
        pe_qkv = {}
        vt_dma = {}
        ret_act = [0] * PAIRS
        pub_v_mark = [0] * PAIRS
        pair_last_pe = [0] * PAIRS
        pe_wo = {}

        def group_stages(hp, ch, pb, waits):
            g = gcnt[0]; gcnt[0] += 1
            g2 = g % 2
            g4 = g % 4
            mask = mfirst_t if ch == 0 else mstd_t
            stages = []

            def _(t):
                for sn, vv in waits:
                    w8(t, sn, vv)
                if g >= 4:
                    w8(t, "s_act", G_act_exp[g - 4])
                    w8(t, "s_dve", G_dve_avc[g - 4])
                t.matmul(ps_sc[:, g4, 0:512], eyeb_t[:, :], mask[:, :],
                         start=True, stop=False, skip_group_check=True)
                for h2 in range(2):
                    if ch == 0:
                        t.matmul(ps_sc[:, g4, 256 * h2:256 * h2 + 256],
                                 ret_qp[:, hp, h2, :], ret_k[:, hp, :],
                                 start=False, stop=True, skip_group_check=True)
                    else:
                        t.matmul(ps_sc[:, g4, 256 * h2:256 * h2 + 256],
                                 qp_sb[:, pb, h2, 128 * ch:128 * (ch + 1)],
                                 k_sb[:, pb, 128 * (ch - 1):128 * (ch - 1) + 256],
                                 start=False, stop=True, skip_group_check=True)
                G_pe_sc[g] = incl(t, "s_pe")
            stages.append(("tensor", _))

            def _(v):
                w8(v, "s_pe", G_pe_sc[g])
                if g >= 4:
                    w8(v, "s_act", G_act_exp[g - 4])
                v.reduce_max(smx_sb[:, g4, 0:2],
                             ps_sc[:, g4, 0:512].rearrange("p (u k) -> p u k", u=2),
                             axis=AX.X, negate=True)
                G_dve_mr[g] = incl(v, "s_dve")
            stages.append(("vector", _))

            def _(s):
                w8(s, "s_dve", G_dve_mr[g])
                if g >= 4:
                    w8(s, "s_pool", G_pool[g - 4])
                for h2 in range(2):
                    s.activation(out=prb_sb[:, g4, h2, :],
                                 in_=ps_sc[:, g4, 256 * h2:256 * h2 + 256],
                                 func=AF.Exp, bias=smx_sb[:, g4, h2:h2 + 1], scale=1.0,
                                 accum_out=smx_sb[:, g4, 2 + h2:3 + h2])
                G_act_exp[g] = incl(s, "s_act")
            stages.append(("scalar", _))

            def _(v):
                w8(v, "s_act", G_act_exp[g])
                v.reciprocal(smx_sb[:, g4, 4:6], smx_sb[:, g4, 2:4])
                G_dve_rc[g] = incl(v, "s_dve")
            stages.append(("vector", _))

            def _(gp):
                w8(gp, "s_dve", G_dve_rc[g])
                for h2 in range(2):
                    gp.tensor_scalar(prb_sb[:, g4, h2, :], prb_sb[:, g4, h2, :],
                                     smx_sb[:, g4, 4 + h2:5 + h2], 0.0, OP.mult, OP.add)
                G_pool[g] = incl(gp, "s_pool")
            stages.append(("gpsimd", _))

            def _(t):
                w8(t, "s_pool", G_pool[g])
                if g >= 2:
                    w8(t, "s_dve", G_dve_ptc[g - 2])
                for j in range(4):
                    t.transpose(ps_pt[:, g2, 128 * j:128 * (j + 1)],
                                prb_sb[:, g4, j // 2, 128 * (j % 2):128 * (j % 2) + 128],
                                eyeb_t[:, :])
                G_pe_pt[g] = incl(t, "s_pe")
            stages.append(("tensor", _))

            def _(v):
                w8(v, "s_pe", G_pe_pt[g])
                if g >= 2:
                    w8(v, "s_pe", G_pe_av[g - 2])
                v.tensor_copy(prt_sb[:, g2, :, :], ps_pt[:, g2, 0:512])
                G_dve_ptc[g] = incl(v, "s_dve")
            stages.append(("vector", _))

            def _(t):
                w8(t, "s_dve", G_dve_ptc[g])
                if ch > 0:
                    w8(t, "s_q", vt_dma[(hp, ch)])
                for h2 in range(2):
                    hb = 64 * h2
                    av = ps_sc[0:64, g4, 128 * h2:128 * h2 + 128]
                    if ch == 0:
                        t.matmul(av, ret_vt[:, hp, 0, hb:hb + 64],
                                 prt_sb[:, g2, 2 * h2, :], start=True, stop=False)
                        t.matmul(av, ret_vt[:, hp, 1, hb:hb + 64],
                                 prt_sb[:, g2, 2 * h2 + 1, :], start=False, stop=True)
                    else:
                        t.matmul(av, vt_sb[:, pb, ch - 1, hb:hb + 64],
                                 prt_sb[:, g2, 2 * h2, :], start=True, stop=False)
                        t.matmul(av, vt_sb[:, pb, ch, hb:hb + 64],
                                 prt_sb[:, g2, 2 * h2 + 1, :], start=False, stop=True)
                G_pe_av[g] = incl(t, "s_pe")
            stages.append(("tensor", _))

            def _(v):
                w8(v, "s_pe", G_pe_av[g])
                for h2 in range(2):
                    hb = 64 * h2
                    v.tensor_copy(sbf[hb:hb + 64, hp, 128 * ch:128 * (ch + 1)],
                                  ps_sc[0:64, g4, 128 * h2:128 * h2 + 128])
                G_dve_avc[g] = incl(v, "s_dve")
            stages.append(("vector", _))
            return stages[:G4LIM]

        _pend = []

        def _emit_batch(batch):
            if not batch:
                return
            ns = max(len(sl) for sl in batch)
            for si in range(min(5, ns)):
                for sl in batch:
                    if si < len(sl):
                        sec(*sl[si])
            for k in range(0, len(batch), 2):
                sub = batch[k:k + 2]
                for si in range(5, ns):
                    for sl in sub:
                        if si < len(sl):
                            sec(*sl[si])

        def emit_group(hp, ch, pb, waits):
            _pend.append(group_stages(hp, ch, pb, waits))
            if len(_pend) == 4:
                _emit_batch(_pend)
                _pend.clear()

        def flush_groups():
            _emit_batch(_pend)
            _pend.clear()

        def _(v):
            v.memset(qp_sb[64:128, :, 0, :], 0.0)
            v.memset(qp_sb[0:64, :, 1, :], 0.0)
            v.memset(ret_k[:, :, 0:128], 0.0)
            v.memset(ret_vt[:, :, 0, :], 0.0)
            incl(v, "s_dve")
        sec("vector", _)
        m_pad = cnt["s_dve"]

        qkv_wm = [0] * (PAIRS + 2)

        def stage_qkv_w(hp):
            hs = hp % 2
            ws = wsem[hs]

            def _(g, hp=hp, hs=hs):
                if hp >= 2:
                    w8(g, "s_pe", pe_qkv[(hp - 2, 1, NP - 1)])
                else:
                    w8(g, "s_dve", ya_done)
                if hp == 0 and tg:
                    w8(g, "s_st", tg)
                for mi, base in ((0, 0), (1, D), (2, 2 * D)):
                    dma(g, ws, [wq_s, wk_s, wv2_s][mi][:, hs, :, :],
                        wqkvt[:, base + 128 * hp:base + 128 * (hp + 1)]
                        .rearrange("(k p) m -> p k m", p=128))
            sec("gpsimd", _)
            qkv_wm[hp] = cnt[ws]

        stage_qkv_w(0)
        stage_qkv_w(1)
        for hp in range(PAIRS):
            pb = hp % 2
            hs = hp % 2
            ws = wsem[hs]
            wm = qkv_wm[hp]

            MI_ORDER = (2, 0, 1)
            for idx, mi in enumerate(MI_ORDER):
                for p in range(NP):
                    sl = slice(p * PIECE, (p + 1) * PIECE)
                    qs = hp * 12 + idx * NP + p
                    qb = qs % 2

                    def _(t, hp=hp, pb=pb, hs=hs, mi=mi, p=p, sl=sl, qs=qs, qb=qb, ws=ws, wm=wm):
                        w8(t, ws, wm)
                        if hp == 0 and qs == 0:
                            w8(t, "s_dve", ya_done)
                        prev = qs - 2
                        if prev >= 0:
                            hpp, rem = divmod(prev, 12)
                            w8(t, "s_act",
                               act_qkv[(hpp, MI_ORDER[rem // NP], rem % NP)])
                        wsrc = [wq_s, wk_s, wv2_s][mi]
                        for kc in range(C):
                            t.matmul(ps_qkv[:, qb, :], wsrc[:, hs, kc, :], opnd[:, kc, sl],
                                     start=(kc == 0), stop=(kc == C - 1))
                        pe_qkv[(hp, mi, p)] = incl(t, "s_pe")
                    sec("tensor", _)

                    def _(s, hp=hp, pb=pb, mi=mi, p=p, sl=sl, qs=qs, qb=qb):
                        w8(s, "s_pe", pe_qkv[(hp, mi, p)])
                        if hp >= 2 and mi == 0 and p == 0:
                            w8(s, "s_pe", pair_last_pe[hp - 2])
                        if mi == 2:
                            if p >= 2:
                                w8(s, "s_q", vt_dma[(hp, 4 * (p - 2) + 3)])
                            elif hp >= 1 and p == 1:
                                w8(s, "s_pub", pub_v_mark[hp - 1])
                            elif hp >= 1:
                                w8(s, "s_q", vt_dma[(hp - 1, 4 * (NP - 2 + p) + 3)])
                            s.activation(out=v_st[:, p % 2, :], in_=ps_qkv[:, qb, :],
                                         func=AF.Copy)
                        elif mi == 0:
                            s.activation(out=qp_sb[0:64, pb, 0, sl],
                                         in_=ps_qkv[0:64, qb, :],
                                         func=AF.Copy, scale=0.125)
                            s.activation(out=qp_sb[64:128, pb, 1, sl],
                                         in_=ps_qkv[64:128, qb, :],
                                         func=AF.Copy, scale=0.125)
                        else:
                            s.activation(out=k_sb[:, pb, sl], in_=ps_qkv[:, qb, :],
                                         func=AF.Copy)
                        act_qkv[(hp, mi, p)] = incl(s, "s_act")
                    sec("scalar", _)

                    if mi == 2:
                        def _(sy, hp=hp, pb=pb, p=p):
                            w8(sy, "s_act", act_qkv[(hp, 2, p)])
                            if hp >= 2 and p == 0:
                                w8(sy, "s_pe", pair_last_pe[hp - 2])
                            for c4 in range(4):
                                ch = 4 * p + c4
                                _i = sy.dma_start_transpose(
                                    vt_sb[:, pb, ch, :],
                                    v_st[:, p % 2, 128 * c4:128 * (c4 + 1)])
                                vt_dma[(hp, ch)] = inc(_i, "s_q", 16)
                        sec("sync", _)

                        if p == NP - 1:
                            def _(g, hp=hp, pb=pb):
                                w8(g, "s_q", vt_dma[(hp, NCH - 1)])
                                pub_v_mark[hp] = dma(
                                    g, "s_pub",
                                    cc_kv_in[:, 1024 + 128 * hp:1024 + 128 * (hp + 1)],
                                    vt_sb[:, pb, NCH - 1, :])
                            sec("gpsimd", _)

            def _(s, hp=hp, pb=pb):
                w8(s, "s_act", act_qkv[(hp, 1, NP - 1)])
                w8(s, "s_dve", m_pad)
                w8(s, "s_q", vt_dma[(hp, 0)])
                s.copy(ret_qp[:, hp, 0, :], qp_sb[:, pb, 0, 0:128])
                s.copy(ret_qp[:, hp, 1, :], qp_sb[:, pb, 1, 0:128])
                s.copy(ret_k[:, hp, 128:256], k_sb[:, pb, 0:128])
                s.copy(ret_vt[:, hp, 1, :], vt_sb[:, pb, 0, :])
                ret_act[hp] = incl(s, "s_act")
            sec("scalar", _)

            def _(g, hp=hp, pb=pb):
                w8(g, "s_act", act_qkv[(hp, 1, NP - 1)])
                dma(g, "s_pub", cc_kv_in[:, 128 * hp:128 * (hp + 1)],
                    k_sb[:, pb, T - 128:T])
            sec("gpsimd", _)

            if hp + 2 < PAIRS:
                stage_qkv_w(hp + 2)

            ready = [("s_act", max(act_qkv[(hp, mi, NP - 1)] for mi in range(3))),
                     ("s_dve", m_pad)]
            if BUILD_PHASES != 34:
                for ch in range(1, NCH):
                    emit_group(hp, ch, pb, ready)
                flush_groups()
            pair_last_pe[hp] = cnt["s_pe"]
            flush()

        if BUILD_PHASES in (34, 35):
            _done35 = max(cnt["s_dve"], 1)
            _p4.close()
            return _finish(_done35)

        std_done = cnt["s_dve"]

        wo_evac = {}
        wo_last_pe = {}
        wo_wm = {}
        wo_seq = [0]
        wo_tiles = [wq_s, wk_s, wv2_s]

        def stage_wo_load(li, m):
            tile = wo_tiles[li % 3]
            slot = (li // 3) % 2
            ws = wsem[li % 2]

            def _(g, li=li, m=m, tile=tile, slot=slot, ws=ws):
                if li >= 6:
                    w8(g, "s_pe", wo_last_pe[li - 6])
                else:
                    w8(g, "s_pe", pe_qkv[(PAIRS - 2 + slot, 1, NP - 1)])
                dma(g, ws, tile[:, slot, :, :],
                    wot[:, 128 * m:128 * (m + 1)].rearrange("(k p) m -> p k m", p=128))
            sec("gpsimd", _)
            wo_wm[li] = cnt[ws]

        def stage_wo_mms(li, m, plist, dep_mark):
            tile = wo_tiles[li % 3]
            slot = (li // 3) % 2
            ws = wsem[li % 2]

            for p in plist:
                sl = slice(p * PIECE, (p + 1) * PIECE)
                qs = wo_seq[0]; wo_seq[0] += 1
                qb = qs % 2

                def _(t, li=li, m=m, tile=tile, slot=slot, p=p, sl=sl, qs=qs,
                      qb=qb, ws=ws, dep_mark=dep_mark):
                    w8(t, ws, wo_wm[li])
                    w8(t, "s_dve", dep_mark)
                    if qs >= 2:
                        w8(t, "s_dve", wo_evac[qs - 2])
                    for kc in range(C):
                        t.matmul(ps_qkv[:, qb, :], tile[:, slot, kc, :],
                                 sbf[:, kc, sl],
                                 start=(kc == 0), stop=(kc == C - 1))
                    pe_wo[qs] = incl(t, "s_pe")
                    wo_last_pe[li] = pe_wo[qs]
                sec("tensor", _)

                def _(v, m=m, p=p, sl=sl, qs=qs, qb=qb):
                    w8(v, "s_pe", pe_wo[qs])
                    v.scalar_tensor_tensor(xf[:, m, sl], ps_qkv[:, qb, :], 1.0,
                                           xf[:, m, sl], OP.mult, OP.add)
                    wo_evac[qs] = incl(v, "s_dve")
                sec("vector", _)

        for li in range(6):
            stage_wo_load(li, li)

        _mq = cnt["s_q"]; _mc = cnt["s_cc"]; _mp = cnt["s_pub"]
        def _(g, _mq=_mq, _mc=_mc, _mp=_mp):
            w8(g, "s_q", _mq)
            w8(g, "s_pub", _mp)
            w8(g, "s_cc", _mc)
            cc = g.collective_compute(
                "AllGather", OP.bypass,
                replica_groups=[list(range(R))],
                ins=[cc_kv_in[:, :]],
                outs=[cc_kv_out[:, :]],
            )
            inc(cc, "s_cc")
        sec("gpsimd", _)
        m_kcc = cnt["s_cc"]

        gsel = ["s_g0", "s_g1"]
        sel_dma = {}
        dve_sel = {}
        for j in range(R):
            for hh in range(2):
                i = 2 * j + hh
                ib = i % 2

                def _(sy, j=j, hh=hh, i=i, ib=ib):
                    if i < 2:
                        w8(sy, "s_cc", m_kcc)
                    if i >= 2:
                        w8(sy, "s_dve", dve_sel[i - 2])
                    sel_dma[i] = dma(sy, gsel[ib], gkv_sb[:, ib, :],
                                     cc_kv_out[128 * j:128 * (j + 1),
                                               1024 * hh:1024 * hh + 1024])
                sec("sync", _)

                _ma = cnt["s_act"]
                def _(v, j=j, hh=hh, i=i, ib=ib, _ma=_ma):
                    w8(v, gsel[ib], sel_dma[i])
                    if i < 2:
                        w8(v, "s_act", _ma)
                    dst = ret_k[:, :, 0:128] if hh == 0 else ret_vt[:, :, 0, :]
                    if i < 2:
                        w8(v, "s_dve", m_pad)
                    v.scalar_tensor_tensor(
                        dst, gkv_sb[:, ib, :].rearrange("p (a b) -> p a b", a=PAIRS),
                        selh_t[:, j:j + 1], dst,
                        OP.mult, OP.add)
                    v.drain()
                    dve_sel[i] = incl(v, "s_dve")
                sec("vector", _)

        m_sel = cnt["s_dve"]
        flush()
        if BUILD_PHASES == 36:
            _p4.close()
            return _finish(m_sel)

        for m in range(C):
            if m >= 6:
                stage_wo_load(m, m)
            stage_wo_mms(m, m, [1, 2, 3], std_done)

        for hp in range(PAIRS):
            emit_group(hp, 0, hp % 2,
                       [("s_dve", m_sel), ("s_act", ret_act[hp])])
        flush_groups()

        if BUILD_PHASES == 37:
            _done37 = cnt["s_dve"]
            _p4.close()
            return _finish(_done37)

        attn_units_done = cnt["s_dve"]
        flush()

        for m in range(C):
            stage_wo_load(C + m, m)
            stage_wo_mms(C + m, m, [0], attn_units_done)
        ph4["done"] = wo_evac[wo_seq[0] - 1]
        flush()
    _p4.close()

    y2_done = ph4["done"]
    tg = tap_xf("y2", y2_done)
    if BUILD_PHASES <= 4:
        return _finish(y2_done)

    yf_done = rms_pass(wffn_t, y2_done, "f", st_gate=tg)
    tg = tap_opnd("yf", yf_done)

    ph5 = {}
    _p5 = ExitStack()
    ps_ga = _p5.enter_context(nc.psum_tensor("ps_ga", [128, 2, PIECE], F32))
    ps_up = _p5.enter_context(nc.psum_tensor("ps_up", [128, 2, PIECE], F32))
    ps_o = _p5.enter_context(nc.psum_tensor("ps_o", [128, 2, PIECE], F32))
    wga_s = _p5.enter_context(nc.sbuf_tensor("wga_s", [128, 2, C, PIECE], BF16))
    wup_s = _p5.enter_context(nc.sbuf_tensor("wup_s", [128, 2, C, PIECE], BF16))
    wou_s = _p5.enter_context(nc.sbuf_tensor("wou_s", [128, 2, 4, D], BF16))
    sgf_s = _p5.enter_context(nc.sbuf_tensor("sgf_s", [128, 2, PIECE], F32))
    if True:

        wsem = ["s_w0", "s_w1"]
        pe_h = {}
        dve_h = {}
        pe_ow = {}
        dve_ow = {}
        for j in range(NJ):
            jb = j % 2
            ws = wsem[jb]

            def _(g, j=j, jb=jb, ws=ws):
                if j >= 2:
                    w8(g, "s_pe", pe_ow[(j - 2, NP - 1, C - 1)])
                else:
                    w8(g, "s_dve", yf_done)
                if j == 0 and tg:
                    w8(g, "s_st", tg)
                dma(g, ws, wga_s[:, jb, :, :],
                    wgatet[:, PIECE * j:PIECE * (j + 1)].rearrange("(k p) m -> p k m", p=128))
                dma(g, ws, wup_s[:, jb, :, :],
                    wupt[:, PIECE * j:PIECE * (j + 1)].rearrange("(k p) m -> p k m", p=128))
                dma(g, ws, wou_s[:, jb, :, :],
                    woutt[PIECE * j:PIECE * (j + 1), :].rearrange("(k p) m -> p k m", p=128))
            sec("gpsimd", _)
            wm = cnt[ws]

            for p in range(NP):
                sl = slice(p * PIECE, (p + 1) * PIECE)
                for jm in range(4):
                    qs = (j * NP + p) * 4 + jm
                    qb = qs % 2

                    def _(t, j=j, jb=jb, p=p, sl=sl, jm=jm, qs=qs, qb=qb, ws=ws, wm=wm):
                        w8(t, ws, wm)
                        if j == 0 and p == 0 and jm == 0:
                            w8(t, "s_dve", yf_done)
                        if qs >= 2:
                            w8(t, "s_dve", dve_h[qs - 2])
                        for kc in range(C):
                            t.matmul(ps_ga[:, qb, :], wga_s[:, jb, kc, 128 * jm:128 * (jm + 1)],
                                     opnd[:, kc, sl], start=(kc == 0), stop=(kc == C - 1))
                        for kc in range(C):
                            t.matmul(ps_up[:, qb, :], wup_s[:, jb, kc, 128 * jm:128 * (jm + 1)],
                                     opnd[:, kc, sl], start=(kc == 0), stop=(kc == C - 1))
                        pe_h[qs] = incl(t, "s_pe")
                    sec("tensor", _)

                    def _(s, qs=qs, qb=qb):
                        w8(s, "s_pe", pe_h[qs])
                        if qs >= 2:
                            w8(s, "s_dve", dve_h[qs - 2])
                        s.activation(out=sgf_s[:, qb, :], in_=ps_ga[:, qb, :], func=AF.Silu)
                        incl(s, "s_act")
                    sec("scalar", _)
                    m_si = cnt["s_act"]

                    def _(v, j=j, jb=jb, p=p, sl=sl, jm=jm, qs=qs, qb=qb, m_si=m_si):
                        w8(v, "s_act", m_si)
                        v.tensor_tensor(sbf[:, 4 * jb + jm, sl], sgf_s[:, qb, :],
                                        ps_up[:, qb, :], OP.mult)
                        dve_h[qs] = incl(v, "s_dve")
                    sec("vector", _)

            for p in range(NP):
                sl = slice(p * PIECE, (p + 1) * PIECE)
                for m in range(C):
                    qs = p * C + m
                    qb = qs % 2

                    def _(t, j=j, jb=jb, p=p, sl=sl, m=m, qs=qs, qb=qb):
                        w8(t, "s_dve", dve_h[(j * NP + p) * 4 + 3])
                        if qs >= 2:
                            w8(t, "s_dve", dve_ow[(j, (qs - 2) // C, (qs - 2) % C)])
                        elif j >= 1:
                            pv = NP * C + qs - 2
                            w8(t, "s_dve", dve_ow[(j - 1, pv // C, pv % C)])
                        for kc in range(4):
                            t.matmul(ps_o[:, qb, :], wou_s[:, jb, kc, 128 * m:128 * (m + 1)],
                                     sbf[:, 4 * jb + kc, sl], start=(kc == 0), stop=(kc == 3))
                        pe_ow[(j, p, m)] = incl(t, "s_pe")
                    sec("tensor", _)

                    def _(v, j=j, p=p, sl=sl, m=m, qs=qs, qb=qb):
                        w8(v, "s_pe", pe_ow[(j, p, m)])
                        v.scalar_tensor_tensor(xf[:, m, sl], ps_o[:, qb, :], 1.0,
                                               xf[:, m, sl], OP.mult, OP.add)
                        dve_ow[(j, p, m)] = incl(v, "s_dve")
                    sec("vector", _)

        ph5["done"] = dve_ow[(NJ - 1, NP - 1, C - 1)]
        flush()
    _p5.close()

    y3_done = ph5["done"]
    tg = tap_xf("y3", y3_done)
    if BUILD_PHASES <= 5:
        return _finish(y3_done)

    _p6 = ExitStack()
    lgw_sb = _p6.enter_context(nc.sbuf_tensor("lgw_sb", [NE, T], F32))
    ltok = _p6.enter_context(nc.sbuf_tensor("ltok", [128, 16, NE], F32))
    gm1 = _p6.enter_context(nc.sbuf_tensor("gm1", [128, 16], F32))
    gm2 = _p6.enter_context(nc.sbuf_tensor("gm2", [128, 16], F32))
    ge = _p6.enter_context(nc.sbuf_tensor("ge", [128, 16, NE], F32))
    gk1 = _p6.enter_context(nc.sbuf_tensor("gk1", [128, 16, NE], F32))
    gk2 = _p6.enter_context(nc.sbuf_tensor("gk2", [128, 16, NE], F32))
    gsc = _p6.enter_context(nc.sbuf_tensor("gsc", [128, 16, NE], F32))
    we0 = _p6.enter_context(nc.sbuf_tensor("we0", [1, 2, T], BF16))
    bem8 = _p6.enter_context(nc.sbuf_tensor("bem8", [NE, D], BF16))
    wgm_s = _p6.enter_context(nc.sbuf_tensor("wgm_s", [128, C, NE], BF16))
    if True:

        _p6a = ExitStack()
        xrl = _p6a.enter_context(nc.sbuf_tensor("xrl", [128, C, PIECE], F32))
        rows = _p6a.enter_context(nc.sbuf_tensor("rows6", [1, 5, PIECE], F32))
        cast2 = _p6a.enter_context(nc.sbuf_tensor("cast2", [128, 2, PIECE], BF16))
        sq2m = _p6a.enter_context(nc.sbuf_tensor("sq2m", [128, 2, PIECE], BF16))
        tmp2 = _p6a.enter_context(nc.sbuf_tensor("tmp2", [128, PIECE], F32))
        ps_mu2 = _p6a.enter_context(nc.psum_tensor("ps_mu2", [1, 2, PIECE], F32))
        ps_ms2 = _p6a.enter_context(nc.psum_tensor("ps_ms2", [1, 2, PIECE], F32))
        ps_c1 = _p6a.enter_context(nc.psum_tensor("ps_c1", [128, PIECE], F32))
        ps_c2 = _p6a.enter_context(nc.psum_tensor("ps_c2", [128, PIECE], F32))
        if True:

            x_mark = [0] * NP
            pe_st2 = [0] * NP
            pe_cs3 = {}
            act_cs3 = {}
            dve_var2 = [0] * NP
            pe_cb2 = [0] * NP
            dve_ap2 = [0] * NP

            for p in range(NP):
                sl = slice(p * PIECE, (p + 1) * PIECE)
                pp = p % 2
                xs_ = ["s_g0", "s_g1"][pp]

                def _(sy, p=p, sl=sl, xs_=xs_):
                    if p >= 1:
                        w8(sy, "s_dve", dve_ap2[p - 1])
                    else:
                        w8(sy, "s_dve", y3_done)
                    if p == 0 and tg:
                        w8(sy, "s_st", tg)
                    x_mark[p] = dma(sy, xs_, xrl[:, :, :],
                                    x_f[:, sl].rearrange("(c p) t -> p c t", p=128))
                sec("sync", _)

                for c in range(C):
                    cs = p * C + c

                    def _(s, p=p, c=c, cs=cs, xs_=xs_):
                        if c == 0:
                            w8(s, xs_, x_mark[p])
                        if cs >= 2:
                            w8(s, "s_pe", pe_cs3[cs - 2])
                        s.activation(out=cast2[:, cs % 2, :], in_=xrl[:, c, :], func=AF.Copy)
                        s.activation(out=sq2m[:, cs % 2, :], in_=xrl[:, c, :], func=AF.Square)
                        act_cs3[cs] = incl(s, "s_act")
                    sec("scalar", _)

                    def _(t, p=p, c=c, cs=cs, pp=pp):
                        w8(t, "s_act", act_cs3[cs])
                        if cs == 0:
                            w8(t, "s_dve", y3_done)
                        if c == 0 and p >= 2:
                            w8(t, "s_dve", dve_var2[p - 2])
                        t.matmul(ps_mu2[0:1, pp, :], ones_b[:, :], cast2[:, cs % 2, :],
                                 start=(c == 0), stop=(c == C - 1))
                        t.matmul(ps_ms2[0:1, pp, :], ones_b[:, :], sq2m[:, cs % 2, :],
                                 start=(c == 0), stop=(c == C - 1))
                        pe_cs3[cs] = incl(t, "s_pe")
                    sec("tensor", _)
                pe_st2[p] = cnt["s_pe"]

                def _(v, p=p, pp=pp):
                    w8(v, "s_pe", pe_st2[p])
                    if p >= 1:
                        w8(v, "s_pe", pe_cb2[p - 1])
                    v.tensor_scalar(rows[0:1, 0, :], ps_mu2[0:1, pp, :], 1.0 / D, None, OP.mult)
                    v.tensor_scalar(rows[0:1, 4, :], ps_ms2[0:1, pp, :], 1.0 / D, None, OP.mult)
                    v.tensor_tensor(rows[0:1, 1, :], rows[0:1, 0, :], rows[0:1, 0, :], OP.mult)
                    v.tensor_tensor(rows[0:1, 1, :], rows[0:1, 4, :], rows[0:1, 1, :], OP.subtract)
                    dve_var2[p] = incl(v, "s_dve")
                sec("vector", _)

                def _(s, p=p):
                    w8(s, "s_dve", dve_var2[p])
                    s.activation(out=rows[0:1, 4, :], in_=rows[0:1, 1, :], func=AF.Ln, bias=eps1_t[0:1, :])
                    s.drain()
                    s.activation(out=rows[0:1, 2, :], in_=rows[0:1, 4, :], func=AF.Exp, scale=-0.5)
                    incl(s, "s_act")
                sec("scalar", _)
                m_r2b = cnt["s_act"]

                def _(v, m_r2b=m_r2b):
                    w8(v, "s_act", m_r2b)
                    v.tensor_tensor(rows[0:1, 0, :], rows[0:1, 0, :], rows[0:1, 2, :], OP.mult)
                    incl(v, "s_dve")
                sec("vector", _)
                m_mr = cnt["s_dve"]

                def _(t, p=p, m_mr=m_mr):
                    w8(t, "s_dve", m_mr)
                    if p >= 1:
                        w8(t, "s_dve", dve_ap2[p - 1])
                    t.matmul(ps_c1[:, :], ones_r[:, :], rows[0:1, 2, :], start=True, stop=True)
                    t.matmul(ps_c2[:, :], ones_r[:, :], rows[0:1, 0, :], start=True, stop=True)
                    pe_cb2[p] = incl(t, "s_pe")
                sec("tensor", _)

                def _(v, p=p, sl=sl):
                    w8(v, "s_pe", pe_cb2[p])
                    for c in range(C):
                        v.tensor_tensor(tmp2[:, :], xrl[:, c, :], ps_c1[:, :], OP.mult)
                        v.tensor_tensor(opnd[:, c, sl], tmp2[:, :], ps_c2[:, :], OP.subtract)
                        v.tensor_tensor(xf[:, c, sl], xf[:, c, sl], xrl[:, c, :], OP.add)
                    dve_ap2[p] = incl(v, "s_dve")
                sec("vector", _)

            xn2_done = dve_ap2[NP - 1]
            flush()
        _p6a.close()

        _p6b = ExitStack()
        ps_big = _p6b.enter_context(nc.psum_tensor("ps_big", [NE, T], F32))
        ps_lt = _p6b.enter_context(nc.psum_tensor("ps_lt", [128, 16, NE], F32))
        if True:

            def _(g):
                w8(g, "s_dve", y3_done)
                dma(g, "s_w0", wgm_s[:, :, :], wgm.rearrange("(k p) m -> p k m", p=128))
            sec("gpsimd", _)
            m_wgm = cnt["s_w0"]

            pe_lg = [0] * NP
            for p in range(NP):
                def _(t, p=p):
                    w8(t, "s_w0", m_wgm)
                    w8(t, "s_dve", xn2_done)
                    for kc in range(C):
                        t.matmul(ps_big[0:NE, p * PIECE:(p + 1) * PIECE], wgm_s[:, kc, :],
                                 opnd[:, kc, p * PIECE:(p + 1) * PIECE],
                                 start=(kc == 0), stop=(kc == C - 1))
                    pe_lg[p] = incl(t, "s_pe")
                sec("tensor", _)

            def _(s):
                for p in range(NP):
                    w8(s, "s_pe", pe_lg[p])
                    s.activation(out=lgw_sb[:, p * PIECE:(p + 1) * PIECE],
                                 in_=ps_big[0:NE, p * PIECE:(p + 1) * PIECE],
                                 func=AF.Identity, bias=bgm_t[:, :], scale=1.0)
                incl(s, "s_act")
            sec("scalar", _)
            m_lg = cnt["s_act"]

            for g16 in range(16):
                def _(t, g16=g16):
                    w8(t, "s_act", m_lg)
                    t.transpose(ps_lt[:, g16, :], lgw_sb[:, 128 * g16:128 * (g16 + 1)],
                                eyef_t[0:NE, 0:NE])
                    incl(t, "s_pe")
                sec("tensor", _)
            m_ltp = cnt["s_pe"]

            def _(s):
                w8(s, "s_pe", m_ltp)
                s.copy(ltok[:, :, :], ps_lt[:, :, :])
                incl(s, "s_act")
            sec("scalar", _)
            m_ltc = cnt["s_act"]

            def _(v):
                w8(v, "s_act", m_ltc)
                v.reduce_max(gm1[:, :], ltok[:, :, :], axis=AX.X)
                v.drain()
                for e in range(NE):
                    v.tensor_tensor(ge[:, :, e], ltok[:, :, e], gm1[:, :], OP.subtract)
                v.drain()
                incl(v, "s_dve")
            sec("vector", _)
            m_gs = cnt["s_dve"]

            def _(s):
                w8(s, "s_dve", m_gs)
                s.activation(out=ge[:, :, :], in_=ge[:, :, :], func=AF.Exp)
                incl(s, "s_act")
            sec("scalar", _)
            m_ge = cnt["s_act"]

            def _(v):
                w8(v, "s_act", m_ge)
                v.reduce_max(gm1[:, :], ge[:, :, :], axis=AX.X)
                v.drain()
                for e in range(NE):
                    v.tensor_tensor(gk1[:, :, e], ge[:, :, e], gm1[:, :], OP.is_ge)
                v.drain()
                v.tensor_tensor(gsc[:, :, :], ge[:, :, :], gk1[:, :, :], OP.mult)
                v.drain()
                v.tensor_tensor(gsc[:, :, :], ge[:, :, :], gsc[:, :, :], OP.subtract)
                v.drain()
                v.reduce_max(gm2[:, :], gsc[:, :, :], axis=AX.X)
                v.drain()
                for e in range(NE):
                    v.tensor_tensor(gk2[:, :, e], gsc[:, :, e], gm2[:, :], OP.is_ge)
                v.drain()
                v.tensor_tensor(gk1[:, :, :], gk1[:, :, :], gk2[:, :, :], OP.add)
                v.drain()
                v.tensor_tensor(gsc[:, :, :], ge[:, :, :], gk1[:, :, :], OP.mult)
                v.drain()
                v.reduce_sum(gm1[:, :], gsc[:, :, :], axis=AX.X)
                v.drain()
                v.reciprocal(gm1[:, :], gm1[:, :])
                v.drain()
                for e in range(NE):
                    v.tensor_tensor(gsc[:, :, e], gsc[:, :, e], gm1[:, :], OP.mult)
                v.drain()
                incl(v, "s_dve")
            sec("vector", _)
            m_w = cnt["s_dve"]

            for g16 in range(16):
                def _(t, g16=g16):
                    w8(t, "s_dve", m_w)
                    t.transpose(ps_big[0:NE, 128 * g16:128 * (g16 + 1)],
                                gsc[:, g16, :], eyef_t[:, :])
                    incl(t, "s_pe")
                sec("tensor", _)
            m_wtp = cnt["s_pe"]

            def _(s):
                w8(s, "s_pe", m_wtp)
                s.copy(lgw_sb[:, :], ps_big[0:NE, :])
                incl(s, "s_act")
            sec("scalar", _)
            m_wem = cnt["s_act"]
            flush()
        _p6b.close()

        _p6c = ExitStack()
        ps_bc = _p6c.enter_context(nc.psum_tensor("ps_bc", [128, 2, PIECE], F32))
        ps_eo = _p6c.enter_context(nc.psum_tensor("ps_eo", [128, 2, PIECE], F32))
        we_s = _p6c.enter_context(nc.sbuf_tensor("we_s", [128, 2, 2, C, PIECE], BF16))
        lgw_b = _p6c.enter_context(nc.sbuf_tensor("lgw_b", [NE, T], BF16))
        bc_sb = _p6c.enter_context(nc.sbuf_tensor("bc_sb", [128, 2, PIECE], BF16))
        if True:

            wsem = ["s_w0", "s_w1"]
            pe_bcm = {}
            act_bc = {}
            dve_xs = {}
            pe_eo = {}
            dve_eo = {}
            we0_mark = [0] * NE
            wh_mark = {}

            def stage_moe_w(e):
                eb = e % 2
                for mh in range(2):
                    hs = wsem[mh]

                    def _(g, e=e, eb=eb, mh=mh, hs=hs):
                        if e >= 2:
                            w8(g, "s_pe", pe_eo[(e - 2, NP - 1, 4 * mh + 3)])
                        else:
                            w8(g, "s_dve", xn2_done)
                        dma(g, hs, we_s[:, eb, mh, :, :],
                            wemt[e, :, PIECE * mh:PIECE * (mh + 1)]
                            .rearrange("(k p) m -> p k m", p=128))
                    sec("gpsimd", _)
                wh_mark[e] = [cnt["s_w0"], cnt["s_w1"]]

            def stage_gate_row(e):
                eb = e % 2

                def _(g, e=e, eb=eb):
                    w8(g, "s_act", m_wem)
                    if e >= 2:
                        w8(g, "s_pe", pe_eo[(e - 2, NP - 1, C - 1)])
                    we0_mark[e] = dma(g, "s_w0", we0[0:1, eb, :],
                                      lgw_sb[e:e + 1, :])
                sec("gpsimd", _)

            stage_moe_w(0)
            stage_moe_w(1)

            def _(g):
                w8(g, "s_act", m_wem)
                dma(g, "s_w0", lgw_b[:, :], lgw_sb[:, :])
                dma(g, "s_w0", bem8[:, :],
                    bem.rearrange("o (e d) -> (o e) d", e=NE))
            sec("gpsimd", _)
            m_lgwb = cnt["s_w0"]
            stage_gate_row(0)
            stage_gate_row(1)

            for e in range(NE):
                eb = e % 2

                for p in range(NP):
                    sl = slice(p * PIECE, (p + 1) * PIECE)
                    bs = e * NP + p
                    bb = bs % 2

                    def _(t, e=e, eb=eb, p=p, sl=sl, bs=bs, bb=bb):
                        w8(t, "s_w0", we0_mark[e])
                        if bs >= 2:
                            w8(t, "s_act", act_bc[bs - 2])
                        t.matmul(ps_bc[:, bb, :], ones_rb[:, :],
                                 we0[0:1, eb, sl], start=True, stop=True)
                        pe_bcm[bs] = incl(t, "s_pe")
                    sec("tensor", _)

                    def _(s, e=e, p=p, sl=sl, bs=bs, bb=bb):
                        w8(s, "s_pe", pe_bcm[bs])
                        if bs >= 2:
                            w8(s, "s_dve", dve_xs[bs - 2])
                        s.activation(out=bc_sb[:, bb, :], in_=ps_bc[:, bb, :],
                                     func=AF.Copy)
                        act_bc[bs] = incl(s, "s_act")
                    sec("scalar", _)

                    def _(v, e=e, p=p, sl=sl, bs=bs, bb=bb):
                        w8(v, "s_act", act_bc[bs])
                        if e >= 1:
                            w8(v, "s_pe", pe_eo[(e - 1, p, C - 1)])
                        if e == 0:
                            w8(v, "s_dve", xn2_done)
                        for kc in range(C):
                            v.tensor_tensor(sbf[:, kc, sl], opnd[:, kc, sl],
                                            bc_sb[:, bb, :], OP.mult)
                        dve_xs[bs] = incl(v, "s_dve")
                    sec("vector", _)

                    for m in range(C):
                        mh = m // 4
                        es = (e * NP + p) * C + m
                        ob = es % 2

                        def _(t, e=e, eb=eb, p=p, sl=sl, m=m, mh=mh, es=es, ob=ob,
                              bs=bs):
                            w8(t, wsem[mh], wh_mark[e][mh])
                            w8(t, "s_dve", dve_xs[bs])
                            if es >= 2:
                                w8(t, "s_dve", dve_eo[es - 2])
                            if e == 0:
                                w8(t, "s_w0", m_lgwb)
                                t.matmul(ps_eo[:, ob, :],
                                         bem8[0:NE, 128 * m:128 * (m + 1)],
                                         lgw_b[0:NE, sl], start=True, stop=False)
                            for kc in range(C):
                                t.matmul(ps_eo[:, ob, :],
                                         we_s[:, eb, mh, kc, 128 * (m - 4 * mh):128 * (m - 4 * mh + 1)],
                                         sbf[:, kc, sl], start=(e > 0 and kc == 0),
                                         stop=(kc == C - 1))
                            pe_eo[(e, p, m)] = incl(t, "s_pe")
                        sec("tensor", _)

                        def _(v, p=p, sl=sl, m=m, es=es, ob=ob):
                            w8(v, "s_pe", pe_eo[(e, p, m)])
                            v.scalar_tensor_tensor(xf[:, m, sl], ps_eo[:, ob, :], 1.0,
                                                   xf[:, m, sl], OP.mult, OP.add)
                            dve_eo[es] = incl(v, "s_dve")
                        sec("vector", _)

                if e + 2 < NE:
                    stage_moe_w(e + 2)
                    stage_gate_row(e + 2)
                flush()
            moe_done = dve_eo[NE * NP * C - 1]
        _p6c.close()
    _p6.close()

    def _(sy):
        w8(sy, "s_dve", moe_done)
        m = 0
        for c in range(C):
            m = dma(sy, "s_st", out_f[128 * c:128 * (c + 1), :], xf[:, c, :])
        w8(sy, "s_st", m)
    sec("sync", _)

    flush()
    ctx.close()
    return nc


_NC_CACHE = {}


def _get_nc():
    key = (DEBUG_TAPS, BUILD_PHASES)
    if key not in _NC_CACHE:
        _NC_CACHE[key] = build_nc()
    return _NC_CACHE[key]


def _tr(a):
    return np.ascontiguousarray(np.asarray(a).T)


def _fm(vec):
    return np.ascontiguousarray(np.asarray(vec).reshape(C, 128).T)


def kernel(L0_attn_x, L0_attn_w_rms_mix, L0_attn_Wg, L0_attn_bg, L0_attn_Wv,
           L0_attn_bv, L0_attn_Wd, L0_attn_bd, L0_attn_w_rms_attn, L0_attn_Wqkv,
           L0_attn_Wo, L0_attn_w_rms_ffn, L0_attn_Wgate, L0_attn_Wup,
           L0_attn_Wout, L0_ffn_Wg, L0_ffn_bg, L0_ffn_We, L0_ffn_be):
    x = np.asarray(L0_attn_x).reshape(16384, D)

    jj = np.arange(128)[:, None]
    kidx = np.arange(256)[None, :]
    allowed = (kidx <= 128 + jj) & (kidx >= jj + 1)
    mask_std = np.tile(np.where(allowed, 0.0, NEG).astype(ml_dtypes.bfloat16), (1, 2))
    mask_first0 = np.tile(np.where(allowed & (kidx >= 128), 0.0, NEG)
                          .astype(ml_dtypes.bfloat16), (1, 2))

    shared = {
        "wgt": _tr(L0_attn_Wg), "wvt": _tr(L0_attn_Wv), "wdt": _tr(L0_attn_Wd),
        "wqkvt": _tr(L0_attn_Wqkv), "wot": _tr(L0_attn_Wo),
        "wgatet": _tr(L0_attn_Wgate), "wupt": _tr(L0_attn_Wup),
        "woutt": _tr(L0_attn_Wout),
        "wemt": np.ascontiguousarray(np.asarray(L0_ffn_We).transpose(0, 2, 1)),
        "wgm": np.ascontiguousarray(np.asarray(L0_ffn_Wg)),
        "wmix": _fm(L0_attn_w_rms_mix), "wattn": _fm(L0_attn_w_rms_attn),
        "wffn": _fm(L0_attn_w_rms_ffn),
        "bg3": _fm(L0_attn_bg), "bv3": _fm(L0_attn_bv), "bd3": _fm(L0_attn_bd),
        "bgm": np.ascontiguousarray(np.asarray(L0_ffn_bg).reshape(NE, 1)),
        "bem": np.ascontiguousarray(np.asarray(L0_ffn_be).reshape(1, NE * D)),
        "eye_bf": np.eye(128, dtype=ml_dtypes.bfloat16),
        "eye_f": np.eye(128, dtype=np.float32),
        "mask_std": mask_std,
    }

    in_maps = []
    for r in range(R):
        im = dict(shared)
        im["x_f"] = np.ascontiguousarray(x[T * r:T * (r + 1), :].T)
        im["mask_first"] = mask_first0 if r == 0 else mask_std
        selc = np.zeros((128, R), dtype=np.float32); selc[:, r] = 1.0
        selh = np.zeros((128, R), dtype=np.float32)
        if r > 0:
            selh[:, r - 1] = 1.0
        im["sel_carry"] = selc
        im["sel_halo"] = selh
        in_maps.append(im)

    nc = _get_nc()
    res = run_bass_kernel_spmd(nc, in_maps, core_ids=list(range(R)))
    out = np.concatenate([np.asarray(res.results[r]["out_f"]).T for r in range(R)],
                         axis=0)
    if DEBUG_TAPS:
        kernel.last_results = res.results
    return np.ascontiguousarray(out.reshape(1, 16384, D)).astype(np.float32)
